# version 53
# baseline (speedup 1.0000x reference)
import contextlib
import numpy as np

import concourse.bass as bass
import concourse.mybir as mybir
from concourse.bass_utils import run_bass_kernel_spmd

F32 = mybir.dt.float32
AF = mybir.ActivationFunctionType
ALU = mybir.AluOpType
AX = mybir.AxisListType

ENGS = ("sync", "tensor", "vector", "scalar", "gpsimd")
DRAIN_ENGS = ("vector", "scalar", "gpsimd")


class Reg:

    __slots__ = ("name", "lw", "readers")

    def __init__(self, name):
        self.name = name
        self.lw = None
        self.readers = {}


class Ops:

    def __init__(self, nc, es):
        self.nc = nc
        self.es = es
        self.q = {e: [] for e in ENGS}
        self.semh = {}
        self.semv = {}
        self.waited = {e: {} for e in ENGS}
        self.dirty = {e: (set(), set()) for e in ENGS}
        self.esem = {}
        for e in ("tensor", "vector", "scalar", "gpsimd"):
            self.esem[e] = self.new_sem("se_" + e)

    def new_sem(self, name):
        h = self.es.enter_context(self.nc.semaphore(name))
        self.semh[name] = h
        self.semv[name] = 0
        return name

    def _wait(self, eng, semname, val, waits):
        if val <= 0:
            return
        if self.waited[eng].get(semname, 0) >= val:
            return
        self.waited[eng][semname] = val
        waits.append((self.semh[semname], val))

    def op(self, eng, builder, reads=(), writes=(), dma=None, sem=None,
           noinc=False):
        waits = []
        drain = False
        wr, rd = self.dirty.get(eng, (set(), set()))
        for r in reads:
            if r.lw is not None:
                peng, sname, val = r.lw
                if peng == eng:
                    if id(r) in wr:
                        drain = True
                else:
                    self._wait(eng, sname, val, waits)
        for w in writes:
            if w.lw is not None:
                peng, sname, val = w.lw
                if peng == eng:
                    if id(w) in wr:
                        drain = True
                else:
                    self._wait(eng, sname, val, waits)
            for key, (sname, val) in w.readers.items():
                if key == eng:
                    if id(w) in rd or id(w) in wr:
                        drain = True
                else:
                    self._wait(eng, sname, val, waits)
        if drain and eng in DRAIN_ENGS:
            wr.clear()
            rd.clear()
        else:
            drain = False
        if dma is not None:
            sem = (dma, 16)
        if sem is not None:
            sname, incval = sem
            self.semv[sname] += incval
            val = self.semv[sname]
            prodkey = sname
        elif noinc and eng != "sync":
            sname = "se_" + eng
            val, incval = self.semv[sname] + 1, 0
            prodkey = eng
        else:
            sname = "se_" + eng if eng != "sync" else None
            if sname is not None:
                self.semv[sname] += 1
                val, incval = self.semv[sname], 1
            else:
                val, incval = 0, 0
            prodkey = eng
        for w in writes:
            w.lw = ((prodkey if sem is None else None), sname, val)
            w.readers = {}
        for r in reads:
            key = sname if sem is not None else eng
            r.readers[key] = (sname, val)
        if eng in DRAIN_ENGS:
            for w in writes:
                wr.add(id(w))
            for r in reads:
                rd.add(id(r))
        semh = (self.semh.get(sname)
                if sname is not None and incval > 0 else None)
        self.q[eng].append((drain, waits, builder, semh, incval))
        if sname is not None and sem is None and incval > 0:
            self.waited[eng][sname] = val

    def barrier(self):
        targets = [(n, v) for n, v in self.semv.items() if v > 0]
        for e in ENGS:
            waits = []
            for n, v in targets:
                if n == "se_" + e:
                    continue
                self._wait(e, n, v, waits)
            dr = e in DRAIN_ENGS
            if waits or dr:
                self.q[e].append((dr, waits, None, None, 0))
            if dr:
                self.dirty[e][0].clear()
                self.dirty[e][1].clear()

    def final_wait(self, eng, semnames):
        waits = []
        for n in semnames:
            self._wait(eng, n, self.semv[n], waits)
        if waits:
            self.q[eng].append((False, waits, None, None, 0))

    def replay(self, block):
        def run(eng):
            def body(h):
                for drain, waits, builder, semh, incval in self.q[eng]:
                    if drain:
                        h.drain()
                    for sh, v in waits:
                        h.wait_ge(sh, v)
                    if builder is not None:
                        inst = builder(h)
                        if semh is not None:
                            inst.then_inc(semh, incval)
            return body

        block.sync(run("sync"))
        block.tensor(run("tensor"))
        block.vector(run("vector"))
        block.scalar(run("scalar"))
        block.gpsimd(run("gpsimd"))


class Cfg:
    def __init__(self, B_loc=2, T=2048, E=1024, H=16, I=2816, n_cores=8):
        self.B_loc, self.T, self.E, self.H, self.I = B_loc, T, E, H, I
        self.n_cores = n_cores
        self.D = 64
        self.EC = E // 128
        self.IC = I // 128
        self.TT = T // 128
        self.TTC = B_loc * T // 128
        self.G = H // 2


C127 = 1.0 / 127.0
MASKVAL = -1.0e30
I32 = mybir.dt.int32
BF16 = mybir.dt.bfloat16


def build(cfg):
    nc = bass.Bass("TRN2", target_bir_lowering=False)
    B, T, E, H, I, D = cfg.B_loc, cfg.T, cfg.E, cfg.H, cfg.I, cfg.D
    EC, IC, TT, TTC, G = cfg.EC, cfg.IC, cfg.TT, cfg.TTC, cfg.G
    NT = B * T
    EBLK = [(i * 512, min(512, E - i * 512)) for i in range((E + 511) // 512)]

    x_in = nc.dram_tensor("x", [NT, E], F32, kind="ExternalInput").ap()
    w1t = nc.dram_tensor("w1t", [E, 3 * E], F32, kind="ExternalInput").ap()
    w2t = nc.dram_tensor("w2t", [E, E], F32, kind="ExternalInput").ap()
    wg_in = nc.dram_tensor("wg", [I, E], F32, kind="ExternalInput").ap()
    wu_in = nc.dram_tensor("wu", [I, E], F32, kind="ExternalInput").ap()
    wd_in = nc.dram_tensor("wd", [E, I], F32, kind="ExternalInput").ap()
    clips = nc.dram_tensor("clips", [1, 2], F32, kind="ExternalInput").ap()
    cost_in = nc.dram_tensor("cost", [T, D], F32, kind="ExternalInput").ap()
    sinft_in = nc.dram_tensor("sinft", [T, D], F32, kind="ExternalInput").ap()
    ident_in = nc.dram_tensor("ident", [128, 128], F32, kind="ExternalInput").ap()
    dmask_in = nc.dram_tensor("dmask", [128, 128], F32, kind="ExternalInput").ap()
    out_d = nc.dram_tensor("out", [NT, E], F32, kind="ExternalOutput").ap()

    x1_d = nc.dram_tensor("x1_d", [NT, E], F32).ap()
    o_d = nc.dram_tensor("o_d", [B, E, T], F32).ap()
    hp_d = nc.dram_tensor("hp_d", [IC, 128, NT], F32).ap()
    wgq_d = nc.dram_tensor("wgq_d", [E, I], BF16).ap()
    wuq_d = nc.dram_tensor("wuq_d", [E, I], BF16).ap()
    wdq_d = nc.dram_tensor("wdq_d", [I, E], BF16).ap()
    wgs_d = nc.dram_tensor("wgs_d", [IC, 128], F32).ap()
    wus_d = nc.dram_tensor("wus_d", [IC, 128], F32).ap()
    wds_d = nc.dram_tensor("wds_d", [EC, 128], F32).ap()
    cc_in_d = [nc.dram_tensor(f"cc_in{i}", [1, 8], F32).ap() for i in range(2)]
    cc_out_d = [nc.dram_tensor(f"cc_out{i}", [1, 8], F32).ap() for i in range(2)]

    es = contextlib.ExitStack()
    with es:
        O = Ops(nc, es)

        def sb(stack, name, shape):
            return stack.enter_context(nc.sbuf_tensor(name, shape, F32))

        c_ident = sb(es, "c_ident", [128, 128])
        c_identb = es.enter_context(nc.sbuf_tensor("c_identb", [128, 128], BF16))
        c_dmaskb = es.enter_context(nc.sbuf_tensor("c_dmaskb", [128, 128], BF16))
        c_cosT = sb(es, "c_cosT", [128, TT, D])
        c_sinfT = sb(es, "c_sinfT", [128, TT, D])
        c_ones = sb(es, "c_ones", [1, 128])
        c_clips = sb(es, "c_clips", [1, 2])
        c_scal = sb(es, "c_scal", [1, 8])
        c_amax = sb(es, "c_amax", [128, 2])
        c_sc1 = sb(es, "c_sc1", [128, 4])
        c_sc2 = sb(es, "c_sc2", [128, 4])
        c_bn = sb(es, "c_bn", [128, 16])
        c_wsc = sb(es, "c_wsc", [128, 2])

        bank = [es.enter_context(nc.psum_tensor(f"pb{i}", [128, 512], F32))
                for i in range(8)]
        Rb = [Reg(f"pb{i}") for i in range(8)]

        R_ident, R_dmask = Reg("ident"), Reg("dmask")
        R_identb = Reg("identb")
        R_dmaskb = Reg("dmaskb")
        R_cosT, R_sinfT = Reg("cosT"), Reg("sinfT")
        R_ones, R_clips, R_scal = Reg("ones"), Reg("clips"), Reg("scal")
        R_amax1, R_amax2 = Reg("amax1"), Reg("amax2")
        R_sc1, R_sc2 = Reg("sc1"), Reg("sc2")
        R_bn, R_row, R_wsc = Reg("bn"), Reg("row"), Reg("wsc")
        R_x_in, R_w1t, R_w2t = Reg("x_in"), Reg("w1t"), Reg("w2t")
        R_wg_in, R_wu_in, R_wd_in = Reg("wg_in"), Reg("wu_in"), Reg("wd_in")
        R_out = Reg("out")
        R_ccin = [Reg("ccin0"), Reg("ccin1")]
        R_ccout = [Reg("ccout0"), Reg("ccout1")]
        R_x1d = [Reg(f"x1d{t}") for t in range(TTC)]
        R_od = [[Reg(f"od{s}_{h}") for h in range(H)] for s in range(B)]
        R_hpd = [Reg(f"hpd{c}") for c in range(IC)]
        R_wgqd = [Reg(f"wgqd{c}") for c in range(IC)]
        R_wuqd = [Reg(f"wuqd{c}") for c in range(IC)]
        R_wdqd = [Reg(f"wdqd{c}") for c in range(EC)]
        R_wgsd, R_wusd, R_wdsd = Reg("wgsd"), Reg("wusd"), Reg("wdsd")

        sC = O.new_sem("ldc")
        O.op("sync", lambda h: h.dma_start(out=c_ident[:], in_=ident_in),
             writes=[R_ident], dma=sC)
        O.op("sync", lambda h: h.dma_start(
            out=c_cosT[:, 0:2, :],
            in_=dmask_in.rearrange("p (a b) -> p a b", b=D)),
            writes=[R_dmask], dma=sC)
        O.op("sync", lambda h: h.dma_start(
            out=c_sinfT[:], in_=sinft_in.rearrange("(tt p) d -> p tt d", p=128)),
            writes=[R_sinfT], dma=sC)
        O.op("sync", lambda h: h.dma_start(out=c_clips[:], in_=clips),
             writes=[R_clips], dma=sC)
        O.op("gpsimd", lambda h: h.memset(c_ones[:], 1.0), writes=[R_ones])
        O.op("gpsimd", lambda h: h.memset(c_amax[:], 0.0),
             writes=[R_amax1, R_amax2])
        for r in (R_ident, R_dmask, R_cosT, R_sinfT, R_clips):
            r.lw = (None, sC, O.semv[sC])
        O.op("vector", lambda h: h.tensor_copy(c_identb[:], c_ident[:]),
             reads=[R_ident], writes=[R_identb])
        O.op("vector", lambda h: h.tensor_copy(
            c_dmaskb[:], c_cosT[:, 0:2, :].rearrange("p a b -> p (a b)")),
            reads=[R_dmask], writes=[R_dmaskb])
        O.op("sync", lambda h: h.dma_start(
            out=c_cosT[:], in_=cost_in.rearrange("(tt p) d -> p tt d", p=128)),
            writes=[R_cosT, R_dmask], dma=sC)
        R_cosT.lw = (None, sC, O.semv[sC])

        sPL = [O.new_sem(f"pl{i}") for i in range(4)]
        sPS = [O.new_sem(f"psm{i}") for i in range(2)]
        sPSC = O.new_sem("psc0")

        def emit_units_cast(ap, Rr, apb, Rrb):
            O.op("vector", lambda h, a=ap, b=apb: h.tensor_copy(b, a.bitcast(I32)),
                 reads=[Rr], writes=[Rrb])
            O.op("vector", lambda h, b=apb: h.tensor_scalar(
                b, b, 127.0, -128.0, ALU.min, ALU.max),
                reads=[Rrb], writes=[Rrb])

        def emit_round_clip(ap, Rr):
            O.op("vector", lambda h, a=ap: h.tensor_copy(a.bitcast(I32), a),
                 reads=[Rr], writes=[Rr])
            O.op("vector", lambda h, a=ap: h.tensor_copy(a, a.bitcast(I32)),
                 reads=[Rr], writes=[Rr])
            O.op("vector", lambda h, a=ap: h.tensor_scalar(
                a, a, 127.0, -128.0, ALU.min, ALU.max),
                reads=[Rr], writes=[Rr])

        def emit_wquant_units(ap, Rr):
            O.op("vector", lambda h, a=ap: h.tensor_reduce(
                c_wsc[:, 0:1], a, AX.X, ALU.max, apply_absolute_value=True),
                reads=[Rr], writes=[R_wsc])
            O.op("vector", lambda h: h.reciprocal(c_wsc[:, 1:2], c_wsc[:, 0:1]),
                 reads=[R_wsc], writes=[R_wsc])
            O.op("vector", lambda h: h.tensor_scalar(
                c_wsc[:, 1:2], c_wsc[:, 1:2], 127.0, None, ALU.mult),
                reads=[R_wsc], writes=[R_wsc])
            O.op("vector", lambda h: h.tensor_scalar(
                c_wsc[:, 0:1], c_wsc[:, 0:1], C127, None, ALU.mult),
                reads=[R_wsc], writes=[R_wsc])
            O.op("vector", lambda h, a=ap: h.tensor_scalar(
                a.bitcast(I32), a, c_wsc[:, 1:2], None, ALU.mult),
                reads=[Rr, R_wsc], writes=[Rr])

        def emit_trans_store(rawb, Rrawb, stg, Rstg, ncc, dstd, rdst,
                             c, ssem, scd, rscd):
            b7 = bank[7].ap().bitcast(BF16)
            for e0 in range(0, ncc, 4):
                n4 = min(4, ncc - e0)
                for k in range(n4):
                    e = e0 + k
                    O.op("tensor", lambda h, i_=rawb[:, e * 128:(e + 1) * 128],
                         o=b7[:, k * 128:(k + 1) * 128]:
                         h.transpose(o, i_, c_identb[:]),
                         reads=[Rrawb, R_identb], writes=[Rb[7]])
                O.op("scalar", lambda h, o=stg[:, e0 * 128:(e0 + n4) * 128],
                     i_=b7[:, 0:n4 * 128]: h.copy(o, i_),
                     reads=[Rb[7]], writes=[Rstg])
            dst = dstd[:, c * 128:(c + 1) * 128].rearrange(
                "(e p) m -> p e m", p=128)
            O.op("gpsimd", lambda h, o=dst,
                 i_=stg[:, 0:ncc * 128].rearrange("p (e m) -> p e m", m=128):
                 h.dma_start(out=o, in_=i_),
                 reads=[Rstg], writes=[rdst], dma=ssem)
            O.op("gpsimd", lambda h, o=scd[c:c + 1, :].rearrange("a p -> p a"),
                 i_=c_wsc[:, 0:1]: h.dma_start(out=o, in_=i_),
                 reads=[R_wsc], writes=[rscd], dma=sPSC)

        prep_jobs = []
        for c in range(IC):
            prep_jobs.append((wg_in[c * 128:(c + 1) * 128, :], R_wg_in, E, EC,
                              wgq_d, R_wgqd[c], c, wgs_d, R_wgsd))
            prep_jobs.append((wu_in[c * 128:(c + 1) * 128, :], R_wu_in, E, EC,
                              wuq_d, R_wuqd[c], c, wus_d, R_wusd))
        for c in range(EC):
            prep_jobs.append((wd_in[c * 128:(c + 1) * 128, :], R_wd_in, I, IC,
                              wdq_d, R_wdqd[c], c, wds_d, R_wdsd))

        def emit_layernorm(x_ap, Rx, y_ap, Ry):
            nhalf = max(1, E // 512)
            w = min(512, E)
            for i in range(nhalf):
                O.op("vector", lambda h, o=c_bn[:, 6 * i:6 * i + 6],
                     a=x_ap[:, i * w:(i + 1) * w]: h.bn_stats(o, a),
                     reads=[Rx], writes=[R_bn])
            O.op("vector", lambda h, n=nhalf: h.bn_aggr(
                c_bn[:, 12:14], c_bn[:, 0:6 * n]),
                reads=[R_bn], writes=[R_bn])
            O.op("vector", lambda h: h.tensor_scalar(
                c_bn[:, 14:15], c_bn[:, 12:13], -1.0, None, ALU.mult),
                reads=[R_bn], writes=[R_bn])
            O.op("vector", lambda h: h.tensor_scalar(
                c_bn[:, 15:16], c_bn[:, 13:14], 1e-5, None, ALU.add),
                reads=[R_bn], writes=[R_bn])
            O.op("scalar", lambda h: h.activation(
                c_bn[:, 15:16], c_bn[:, 15:16], AF.Sqrt),
                reads=[R_bn], writes=[R_bn])
            O.op("vector", lambda h: h.reciprocal(c_bn[:, 15:16], c_bn[:, 15:16]),
                 reads=[R_bn], writes=[R_bn])
            O.op("vector", lambda h, o=y_ap, a=x_ap: h.tensor_scalar(
                o, a, c_bn[:, 14:15], c_bn[:, 15:16], ALU.add, ALU.mult),
                reads=[Rx, R_bn], writes=[Ry])

        es_w2keep = contextlib.ExitStack()
        w2sh = es_w2keep.enter_context(nc.sbuf_tensor("w2sh", [128, EC, E], BF16))
        w2sl = es_w2keep.enter_context(nc.sbuf_tensor("w2sl", [128, EC, E], BF16))
        R_w2sh, R_w2sl = Reg("w2sh"), Reg("w2sl")
        es_w2p = contextlib.ExitStack()
        w2stg = [sb(es_w2p, f"w2stg{i}", [128, 1, E]) for i in range(2)]
        R_w2stg = [Reg("w2stg0"), Reg("w2stg1")]
        sW2 = [O.new_sem("w2s0"), O.new_sem("w2s1")]
        for e0 in range(EC):
            sl = e0 % 2
            src = w2t[e0 * 128:(e0 + 1) * 128, :].rearrange(
                "(e p) m -> p e m", p=128)
            O.op("sync", lambda h, o=w2stg[sl][:], i_=src:
                 h.dma_start(out=o, in_=i_),
                 reads=[R_w2t], writes=[R_w2stg[sl]], dma=sW2[sl])
            O.op("vector", lambda h, o=w2sh[:, e0:e0 + 1, :],
                 a=w2stg[sl][:]: h.tensor_copy(o, a),
                 reads=[R_w2stg[sl]], writes=[R_w2sh])
            O.op("vector", lambda h, o=w2sl[:, e0:e0 + 1, :],
                 a=w2stg[sl][:], b=w2sh[:, e0:e0 + 1, :]:
                 h.tensor_tensor(o, a, b, ALU.subtract),
                 reads=[R_w2stg[sl], R_w2sh], writes=[R_w2sl])
        O.barrier()
        es_w2p.close()

        es_ln1 = contextlib.ExitStack()
        ln1h = es_ln1.enter_context(nc.sbuf_tensor("ln1h", [128, EC, T], BF16))
        ln1l = es_ln1.enter_context(nc.sbuf_tensor("ln1l", [128, EC, T], BF16))
        R_ln1h, R_ln1l = Reg("ln1h"), Reg("ln1l")
        xa = [sb(es_ln1, f"xa{i}", [128, E]) for i in range(3)]
        R_xa = [Reg(f"xa{i}") for i in range(3)]
        sXA = [O.new_sem(f"xa{i}") for i in range(3)]
        xab = [es_ln1.enter_context(
            nc.sbuf_tensor(f"xab{i}", [128, E], BF16)) for i in range(2)]
        R_xab = [Reg(f"xab{i}") for i in range(2)]

        def emit_ln1_dve(s2, tt):
            sl = tt % 3
            sl2 = tt % 2
            gt = s2 * T + tt * 128
            O.op("sync", lambda h, o=xa[sl][:], i_=x_in[gt:gt + 128, :]:
                 h.dma_start(out=o, in_=i_),
                 reads=[R_x_in], writes=[R_xa[sl]], dma=sXA[sl])
            emit_layernorm(xa[sl][:], R_xa[sl], xa[sl][:], R_xa[sl])
            O.op("vector", lambda h, o=xab[sl2][:], a=xa[sl][:]:
                 h.tensor_copy(o, a),
                 reads=[R_xa[sl]], writes=[R_xab[sl2]])
            O.op("vector", lambda h, o=xa[sl][:], a=xa[sl][:], b=xab[sl2][:]:
                 h.tensor_tensor(o, a, b, ALU.subtract),
                 reads=[R_xa[sl], R_xab[sl2]], writes=[R_xa[sl]])

        def emit_ln1_pe(s2, tt):
            sl = tt % 3
            sl2 = tt % 2
            for (srcb, rsrc, dstt, rdst) in (
                    (xab[sl2], R_xab[sl2], ln1h, R_ln1h),
                    (None, R_xa[sl], ln1l, R_ln1l)):
                for half in range((EC + 3) // 4):
                    n4 = min(4, EC - half * 4)
                    bk = 4 + half % 2
                    bb = bank[bk].ap().bitcast(BF16)
                    for k in range(n4):
                        e = half * 4 + k
                        if srcb is not None:
                            i_ap = srcb[:, e * 128:(e + 1) * 128]
                            idn, ridn = c_identb[:], R_identb
                        else:
                            i_ap = xa[sl][:, e * 128:(e + 1) * 128]
                            idn, ridn = c_ident[:], R_ident
                        O.op("tensor", lambda h, i_=i_ap,
                             o=(bb[:, k * 128:(k + 1) * 128] if srcb is not None
                                else bank[bk][:, k * 128:(k + 1) * 128]),
                             idn=idn: h.transpose(o, i_, idn),
                             reads=[rsrc, ridn], writes=[Rb[bk]])
                    dst = dstt[:, half * 4:half * 4 + n4,
                               tt * 128:(tt + 1) * 128]
                    if srcb is not None:
                        src_ap = bb[:, 0:n4 * 128]
                    else:
                        src_ap = bank[bk][:, 0:n4 * 128]
                    O.op("scalar", lambda h, o=dst, i_=src_ap
                         .rearrange("p (e m) -> p e m", m=128): h.copy(o, i_),
                         reads=[Rb[bk]], writes=[rdst])

        for s in range(B):
            es_at = contextlib.ExitStack()
            w1s = sb(es_at, f"s{s}w1s", [128, 4, 384])
            R_w1s = [Reg(f"w1s{i}") for i in range(4)]
            w1gh = es_at.enter_context(
                nc.sbuf_tensor(f"s{s}w1gh", [128, EC, 384], BF16))
            w1gl = es_at.enter_context(
                nc.sbuf_tensor(f"s{s}w1gl", [128, EC, 384], BF16))
            R_w1gh, R_w1gl = Reg("w1gh"), Reg("w1gl")
            sW1 = [O.new_sem(f"s{s}w1g{i}") for i in range(4)]
            qkv_tok = sb(es_at, f"s{s}qkv", [128, TT, 256])
            R_qkvt = [Reg(f"qkv{t}") for t in range(TT)]
            rope_tmp = sb(es_at, f"s{s}rtmp", [128, 128])
            R_rtmp = Reg("rtmp")
            sqscr = sb(es_at, f"s{s}sqs", [128, 64])
            R_sqscr = Reg("sqscr")
            vbuf = [sb(es_at, f"s{s}vb{i}", [128, TT, 130]) for i in range(2)]
            R_vbuf = [Reg("vbuf0"), Reg("vbuf1")]
            for i in range(2):
                O.op("gpsimd", lambda h, o=vbuf[i][:, :, 64:65]:
                     h.memset(o, 1.0), writes=[R_vbuf[i]])
                O.op("gpsimd", lambda h, o=vbuf[i][:, :, 129:130]:
                     h.memset(o, 1.0), writes=[R_vbuf[i]])
            qkh = {"q": es_at.enter_context(nc.sbuf_tensor(
                       f"s{s}qh", [128, T], BF16)),
                   "k": es_at.enter_context(nc.sbuf_tensor(
                       f"s{s}kh", [128, T], BF16))}
            qkl = {"q": es_at.enter_context(nc.sbuf_tensor(
                       f"s{s}ql", [128, T], BF16)),
                   "k": es_at.enter_context(nc.sbuf_tensor(
                       f"s{s}kl", [128, T], BF16))}
            R_qkh = {"q": [Reg("qh0"), Reg("qh1")],
                     "k": [Reg("kh0"), Reg("kh1")]}
            R_qkl = {"q": [Reg("ql0"), Reg("ql1")],
                     "k": [Reg("kl0"), Reg("kl1")]}
            ptmp = [sb(es_at, f"s{s}p{i}", [128, 512]) for i in range(3)]
            R_ptmp = [Reg(f"ptmp{i}") for i in range(3)]
            rc1 = sb(es_at, f"s{s}rc", [1, 2, 512])
            R_rc1 = [Reg("rc0"), Reg("rc1")]
            negcb = sb(es_at, f"s{s}ngc", [128, 2])
            R_negcb = [Reg("ngc0"), Reg("ngc1")]
            nsq = sb(es_at, f"s{s}nsq", [128, 4, TT])
            nmx = sb(es_at, f"s{s}nmx", [128, 4])
            stab = sb(es_at, f"s{s}stb", [4, 8])
            R_nsq, R_nmx, R_stab = Reg("nsq"), Reg("nmx"), Reg("stab")
            ostg = [sb(es_at, f"s{s}o{i}", [64, 512]) for i in range(2)]
            R_ostg = [Reg("ostg0"), Reg("ostg1")]
            sOS = [O.new_sem(f"s{s}ost0"), O.new_sem(f"s{s}ost1")]
            NQB = (T + 511) // 512

            if s == 0:
                for tt in range(TT):
                    emit_ln1_dve(0, tt)
                    if tt >= 1:
                        emit_ln1_pe(0, tt - 1)
                emit_ln1_pe(0, TT - 1)

            def emit_qkv_load_split(g):
                for e0 in range(EC):
                    ws = e0 % 4
                    src = w1t[e0 * 128:(e0 + 1) * 128,
                              g * 384:(g + 1) * 384].rearrange(
                        "(e p) m -> p e m", p=128)
                    O.op("sync", lambda h, o=w1s[:, ws:ws + 1, :], i_=src:
                         h.dma_start(out=o, in_=i_),
                         reads=[R_w1t], writes=[R_w1s[ws]], dma=sW1[ws])
                    O.op("vector", lambda h, o=w1gh[:, e0:e0 + 1, :],
                         a=w1s[:, ws:ws + 1, :]: h.tensor_copy(o, a),
                         reads=[R_w1s[ws]], writes=[R_w1gh])
                    O.op("vector", lambda h, o=w1gl[:, e0:e0 + 1, :],
                         a=w1s[:, ws:ws + 1, :], b=w1gh[:, e0:e0 + 1, :]:
                         h.tensor_tensor(o, a, b, ALU.subtract),
                         reads=[R_w1s[ws], R_w1gh], writes=[R_w1gl])

            def emit_qkv_tt(g, tt):
                bk = 6
                ops = []
                for e in range(EC):
                    ops.append((ln1h, R_ln1h, w1gh, R_w1gh, e))
                    ops.append((ln1h, R_ln1h, w1gl, R_w1gl, e))
                    ops.append((ln1l, R_ln1l, w1gh, R_w1gh, e))
                for i, (lt, rl, wt, rw, e) in enumerate(ops):
                    spv = (i == len(ops) - 1)
                    O.op("tensor", lambda h, o=bank[bk][:, 0:384],
                         l=lt[:, e, tt * 128:(tt + 1) * 128],
                         r=wt[:, e, :], st=(i == 0), sp=(i == len(ops) - 1):
                         h.matmul(o, l, r, start=st, stop=sp),
                         reads=[rl, rw], writes=[Rb[bk]], noinc=not spv)
                gv = g % 2
                O.op("scalar", lambda h, o=qkv_tok[:, tt, :],
                     i_=bank[bk][:, 0:256]: h.copy(o, i_),
                     reads=[Rb[bk]], writes=[R_qkvt[tt]])
                O.op("scalar", lambda h, o=vbuf[gv][:, tt, 0:64],
                     i_=bank[bk][:, 256:320]: h.copy(o, i_),
                     reads=[Rb[bk]], writes=[R_vbuf[gv]])
                O.op("scalar", lambda h, o=vbuf[gv][:, tt, 65:129],
                     i_=bank[bk][:, 320:384]: h.copy(o, i_),
                     reads=[Rb[bk]], writes=[R_vbuf[gv]])

            def emit_rope_tile(g, tt):
                for idx in range(4):
                    c0 = idx * 64
                    src = qkv_tok[:, tt, c0:c0 + 64]
                    t1 = rope_tmp[:, 0:64]
                    t2 = rope_tmp[:, 64:128]
                    O.op("vector", lambda h, o=t1, a=src,
                         b=c_cosT[:, tt, :]:
                         h.tensor_tensor(o, a, b, ALU.mult),
                         reads=[R_qkvt[tt], R_cosT], writes=[R_rtmp])
                    O.op("vector", lambda h, o=t2[:, 0:32],
                         a=qkv_tok[:, tt, c0 + 32:c0 + 64],
                         b=c_sinfT[:, tt, 0:32]:
                         h.tensor_tensor(o, a, b, ALU.mult),
                         reads=[R_qkvt[tt], R_sinfT], writes=[R_rtmp])
                    O.op("vector", lambda h, o=t2[:, 32:64],
                         a=qkv_tok[:, tt, c0:c0 + 32],
                         b=c_sinfT[:, tt, 32:64]:
                         h.tensor_tensor(o, a, b, ALU.mult),
                         reads=[R_qkvt[tt], R_sinfT], writes=[R_rtmp])
                    O.op("vector", lambda h, o=src, a=t1, b=t2:
                         h.tensor_tensor(o, a, b, ALU.add),
                         reads=[R_rtmp], writes=[R_qkvt[tt]])
            def emit_stab_sq(g, tt):
                for idx in range(4):
                    O.op("scalar", lambda h, o=sqscr[:],
                         a=qkv_tok[:, tt, idx * 64:(idx + 1) * 64],
                         m=nsq[:, idx, tt:tt + 1]:
                         h.activation(o, a, AF.Square, accum_out=m),
                         reads=[R_qkvt[tt]], writes=[R_sqscr, R_nsq])

            def emit_stab(g):
                gp = g % 2
                for idx in range(4):
                    O.op("vector", lambda h, o=nmx[:, idx:idx + 1],
                         a=nsq[:, idx, :]: h.tensor_reduce(o, a, AX.X, ALU.max),
                         reads=[R_nsq], writes=[R_nmx])
                O.op("tensor", lambda h, a=nmx[:, 0:4]: h.transpose(
                    bank[2][0:4, 0:128], a, c_ident[:]),
                    reads=[R_nmx, R_ident], writes=[Rb[2]])
                O.op("vector", lambda h, o=stab[0:4, 0:1]: h.tensor_reduce(
                    o, bank[2][0:4, 0:128], AX.X, ALU.max),
                    reads=[Rb[2]], writes=[R_stab])
                O.op("tensor", lambda h, a=stab[0:4, 0:1]: h.transpose(
                    bank[2][0:1, 0:4], a, c_ident[0:4, 0:4]),
                    reads=[R_stab, R_ident], writes=[Rb[2]])
                O.op("vector", lambda h, o=stab[0:1, 0:4]: h.tensor_copy(
                    o, bank[2][0:1, 0:4]),
                    reads=[Rb[2]], writes=[R_stab])
                O.op("vector", lambda h, o=stab[0:1, 4:5], a=stab[0:1, 0:1],
                     b=stab[0:1, 1:2]: h.tensor_tensor(o, a, b, ALU.mult),
                     reads=[R_stab], writes=[R_stab])
                O.op("vector", lambda h, o=stab[0:1, 5:6], a=stab[0:1, 2:3],
                     b=stab[0:1, 3:4]: h.tensor_tensor(o, a, b, ALU.mult),
                     reads=[R_stab], writes=[R_stab])
                O.op("vector", lambda h, o=stab[0:1, 4:5], a=stab[0:1, 4:5],
                     b=stab[0:1, 5:6]: h.tensor_tensor(o, a, b, ALU.max),
                     reads=[R_stab], writes=[R_stab])
                O.op("scalar", lambda h, o=stab[0:1, 4:5]: h.activation(
                    o, o, AF.Sqrt),
                    reads=[R_stab], writes=[R_stab])
                O.op("vector", lambda h, o=stab[0:1, 6:7], a=stab[0:1, 4:5]:
                     h.tensor_scalar(o, a, 0.1255, -40.0, ALU.mult, ALU.add),
                     reads=[R_stab], writes=[R_stab])
                O.op("scalar", lambda h, o=stab[0:1, 6:7]: h.activation(
                    o, o, AF.Relu),
                    reads=[R_stab], writes=[R_stab])
                O.op("vector", lambda h, o=stab[0:1, 6:7]: h.tensor_scalar(
                    o, o, -1.0, None, ALU.mult),
                    reads=[R_stab], writes=[R_stab])
                O.op("tensor", lambda h, r=stab[0:1, 6:7]: h.matmul(
                    bank[2][:, 0:1], c_ones[0:1, 0:128], r,
                    start=True, stop=True),
                    reads=[R_ones, R_stab], writes=[Rb[2]])
                O.op("scalar", lambda h, o=negcb[:, gp:gp + 1]:
                     h.copy(o, bank[2][:, 0:1]),
                     reads=[Rb[2]], writes=[R_negcb[gp]])

            def emit_qk_trans_group(g2, idx, t0):
                qk = "q" if idx % 2 == 0 else "k"
                half = idx // 2
                r0 = 64 * half
                hbuf, hreg = qkh[qk], R_qkh[qk][half]
                lbuf, lreg = qkl[qk], R_qkl[qk][half]
                c0 = idx * 64
                qsc = 0.125 if qk == "q" else None
                n4 = min(4, TT - t0)
                for k in range(n4):
                    tt = t0 + k
                    O.op("tensor", lambda h,
                         o=bank[7][0:64, k * 128:(k + 1) * 128],
                         i_=qkv_tok[:, tt, c0:c0 + 64]:
                         h.transpose(o, i_, c_ident[:]),
                         reads=[R_qkvt[tt], R_ident], writes=[Rb[7]])
                if qsc is None:
                    O.op("scalar", lambda h,
                         o=hbuf[r0:r0 + 64, t0 * 128:(t0 + n4) * 128],
                         i_=bank[7][0:64, 0:n4 * 128]: h.copy(o, i_),
                         reads=[Rb[7]], writes=[hreg])
                    O.op("vector", lambda h,
                         o=lbuf[r0:r0 + 64, t0 * 128:(t0 + n4) * 128],
                         a=bank[7][0:64, 0:n4 * 128],
                         b=hbuf[r0:r0 + 64, t0 * 128:(t0 + n4) * 128]:
                         h.tensor_tensor(o, a, b, ALU.subtract),
                         reads=[Rb[7], hreg], writes=[lreg])
                else:
                    O.op("scalar", lambda h,
                         o=hbuf[r0:r0 + 64, t0 * 128:(t0 + n4) * 128],
                         i_=bank[7][0:64, 0:n4 * 128], sc=qsc:
                         h.activation(o, i_, AF.Copy, bias=0.0,
                                      scale=sc),
                         reads=[Rb[7]], writes=[hreg])
                    O.op("vector", lambda h,
                         o=lbuf[r0:r0 + 64, t0 * 128:(t0 + n4) * 128],
                         a=bank[7][0:64, 0:n4 * 128],
                         b=hbuf[r0:r0 + 64, t0 * 128:(t0 + n4) * 128],
                         sc=qsc: h.scalar_tensor_tensor(
                             o, a, sc, b, ALU.mult, ALU.subtract),
                         reads=[Rb[7], hreg], writes=[lreg])

            att_chunks = []
            for kt in range(TT):
                for qb in range((kt * 128) // 512, NQB):
                    att_chunks.append((kt, qb))
            att_stops = {}
            for ci, (kt, qb) in enumerate(att_chunks):
                if kt == (min(T, qb * 512 + 512) - 1) // 128:
                    att_stops[ci] = qb

            def emit_attention_head(g, hh, fill, pend):
                h_abs = 2 * g + hh
                gp = g % 2
                nch = len(att_chunks)

                def s_group(ci):
                    kt, qb = att_chunks[ci]
                    q0 = qb * 512
                    q1 = min(T, q0 + 512)
                    qlo = max(q0, kt * 128)
                    w = q1 - qlo
                    bk = 4 + ci % 2
                    diag = (qlo == kt * 128)
                    r0 = 64 * hh
                    passes = ((qkh["k"], R_qkh["k"][hh],
                               qkh["q"], R_qkh["q"][hh]),
                              (qkh["k"], R_qkh["k"][hh],
                               qkl["q"], R_qkl["q"][hh]),
                              (qkl["k"], R_qkl["k"][hh],
                               qkh["q"], R_qkh["q"][hh]))
                    for i, (kb, rk, qv, rq) in enumerate(passes):
                        sp = (i == 2 and not diag)
                        O.op("tensor", lambda h, o=bank[bk][:, 0:w],
                             l=kb[r0:r0 + 64, kt * 128:(kt + 1) * 128],
                             r=qv[r0:r0 + 64, qlo:q1], st=(i == 0), sp=sp:
                             h.matmul(o, l, r, start=st, stop=sp),
                             reads=[rk, rq], writes=[Rb[bk]], noinc=not sp)
                    if diag:
                        O.op("tensor", lambda h, o=bank[bk][:, 0:128]:
                             h.matmul(o, c_identb[:], c_dmaskb[:],
                                      start=False, stop=True),
                             reads=[R_identb, R_dmaskb], writes=[Rb[bk]])
                    ps = ci % 3
                    O.op("scalar", lambda h, o=ptmp[ps][:, 0:w],
                         a=bank[bk][:, 0:w], ng=negcb[:, gp:gp + 1]:
                         h.activation(o, a, AF.Exp, bias=ng, scale=1.0),
                         reads=[Rb[bk], R_negcb[gp]],
                         writes=[R_ptmp[ps]])

                def pv(ci):
                    kt, qb = att_chunks[ci]
                    q0 = qb * 512
                    q1 = min(T, q0 + 512)
                    qlo = max(q0, kt * 128)
                    ps = ci % 3
                    sp = ci in att_stops
                    O.op("tensor", lambda h,
                         o=bank[qb][0:65, qlo - q0:q1 - q0],
                         l=vbuf[g % 2][:, kt, hh * 65:hh * 65 + 65],
                         r=ptmp[ps][:, 0:q1 - qlo],
                         st=(kt == 0), sp=sp:
                         h.matmul(o, l, r, start=st, stop=sp),
                         reads=[R_vbuf[g % 2], R_ptmp[ps]], writes=[Rb[qb]],
                         noinc=not sp)

                for ci in range(nch):
                    s_group(ci)
                    if ci >= 2:
                        pv(ci - 2)
                        if (ci - 2) in att_stops:
                            qb = att_stops[ci - 2]
                            ev_recip(qb)
                            pend.append([-1, qb, h_abs])
                    if ci % 5 in (1, 3):
                        fill()
                    for it in pend:
                        if it[0] == 0:
                            ev_ocopy(it[1])
                        elif it[0] == 1:
                            ev_bcast(it[1])
                        elif it[0] == 2:
                            ev_out(it[1], it[2])
                        it[0] += 1
                    pend[:] = [it for it in pend if it[0] < 3]
                flush_pend(pend)
                for cj in (nch - 2, nch - 1):
                    if cj >= 0:
                        pv(cj)
                        if cj in att_stops:
                            ev_recip(att_stops[cj])
                            pend.append([-1, att_stops[cj], h_abs])

            def ev_recip(qb):
                w = min(T, qb * 512 + 512) - qb * 512
                sl = qb % 2
                O.op("vector", lambda h, o=rc1[0:1, sl, 0:w],
                     a=bank[qb][64:65, 0:w]: h.reciprocal(o, a),
                     reads=[Rb[qb]], writes=[R_rc1[sl]])

            def ev_bcast(qb):
                w = min(T, qb * 512 + 512) - qb * 512
                sl = qb % 2
                O.op("tensor", lambda h, o=bank[7][0:64, 0:w],
                     l=c_ones[0:1, 0:64], r=rc1[0:1, sl, 0:w]:
                     h.matmul(o, l, r, start=True, stop=True),
                     reads=[R_rc1[sl], R_ones], writes=[Rb[7]])

            def ev_ocopy(qb):
                q0 = qb * 512
                w = min(T, q0 + 512) - q0
                sl = qb % 2
                O.op("scalar", lambda h, o=ostg[sl][0:64, 0:w],
                     i_=bank[qb][0:64, 0:w]: h.copy(o, i_),
                     reads=[Rb[qb]], writes=[R_ostg[sl]])

            def ev_out(qb, h_abs):
                q0 = qb * 512
                w = min(T, q0 + 512) - q0
                sl = qb % 2
                O.op("vector", lambda h, o=ostg[sl][0:64, 0:w],
                     b=bank[7][0:64, 0:w]:
                     h.tensor_tensor(o, o, b, ALU.mult),
                     reads=[R_ostg[sl], Rb[7]], writes=[R_ostg[sl]])
                O.op("gpsimd", lambda h,
                     o=o_d[s, h_abs * 64:(h_abs + 1) * 64, q0:q0 + w],
                     i_=ostg[sl][0:64, 0:w]: h.dma_start(out=o, in_=i_),
                     reads=[R_ostg[sl]], writes=[R_od[s][h_abs]],
                     dma=sOS[sl])

            def flush_pend(pend):
                for it in pend:
                    st = it[0]
                    if st <= 0:
                        ev_ocopy(it[1])
                    if st <= 1:
                        ev_bcast(it[1])
                    ev_out(it[1], it[2])
                pend[:] = []

            def emit_attention_pair(g):
                fillB = [(g, idx, t0) for t0 in range(0, TT, 4)
                         for idx in (2, 3)]
                fillR = []
                if g + 1 < G:
                    fillR += [("qkv", tt) for tt in range(TT)]
                    fillR += [("ta", idx, t0) for t0 in range(0, TT, 4)
                              for idx in (0, 1)]

                def pop_job():
                    if fillB:
                        g2, idx, t0 = fillB.pop(0)
                        emit_qk_trans_group(g2, idx, t0)
                    elif fillR:
                        j = fillR.pop(0)
                        if j[0] == "qkv":
                            emit_qkv_tt(g + 1, j[1])
                            emit_stab_sq(g + 1, j[1])
                            emit_rope_tile(g + 1, j[1])
                        else:
                            emit_qk_trans_group(g + 1, j[1], j[2])

                pend = []
                emit_attention_head(g, 0, pop_job, pend)
                while fillB:
                    pop_job()
                emit_attention_head(g, 1, pop_job, pend)
                while fillR:
                    pop_job()
                flush_pend(pend)

            emit_qkv_load_split(0)
            for tt in range(TT):
                emit_qkv_tt(0, tt)
                emit_stab_sq(0, tt)
                emit_rope_tile(0, tt)
            for t0 in range(0, TT, 4):
                for idx in (0, 1):
                    emit_qk_trans_group(0, idx, t0)
            emit_stab(0)
            for g in range(G):
                if g + 1 < G:
                    emit_qkv_load_split(g + 1)
                emit_attention_pair(g)
                if g + 1 < G:
                    emit_stab(g + 1)
            O.barrier()
            es_at.close()

            es_w2 = contextlib.ExitStack()
            oblk = [sb(es_w2, f"s{s}ob{i}", [128, EC, 128]) for i in range(3)]
            R_oblk = [Reg(f"ob{i}") for i in range(3)]
            oblkh = [es_w2.enter_context(nc.sbuf_tensor(
                f"s{s}obh{i}", [128, EC, 128], BF16)) for i in range(2)]
            oblkl = [es_w2.enter_context(nc.sbuf_tensor(
                f"s{s}obl{i}", [128, EC, 128], BF16)) for i in range(2)]
            R_oblkh = [Reg("obh0"), Reg("obh1")]
            R_oblkl = [Reg("obl0"), Reg("obl1")]
            sOB = [O.new_sem(f"s{s}ob{i}") for i in range(3)]
            xw = [sb(es_w2, f"s{s}xw{i}", [128, E]) for i in range(2)]
            R_xw = [Reg("xw0"), Reg("xw1")]
            sXW = [O.new_sem(f"s{s}xw0"), O.new_sem(f"s{s}xw1")]
            x1t = [sb(es_w2, f"s{s}x1t{i}", [128, E]) for i in range(2)]
            R_x1t = [Reg("x1t0"), Reg("x1t1")]
            yt = [sb(es_w2, f"s{s}yt{i}", [128, E]) for i in range(2)]
            R_yt = [Reg("yt0"), Reg("yt1")]
            sX1 = [O.new_sem(f"s{s}x10"), O.new_sem(f"s{s}x11")]
            pdr = [sb(es_w2, f"s{s}pdr{i}", [128, max(E, I)])
                   for i in range(2)]
            pdrb = [es_w2.enter_context(
                nc.sbuf_tensor(f"s{s}pdrb{i}", [128, max(E, I)], BF16))
                for i in range(2)]
            pds = es_w2.enter_context(
                nc.sbuf_tensor(f"s{s}pds", [128, max(EC, IC) * 128], BF16))
            Rpdr = [Reg("pdr0"), Reg("pdr1")]
            Rpdrb = [Reg("pdrb0"), Reg("pdrb1")]
            Rpds = Reg("pds")

            npj = len(prep_jobs)
            jobs = prep_jobs[s * npj // B:(s + 1) * npj // B]
            njobs = len(jobs)
            for tt in range(TT):
                sl = tt % 2
                s4 = tt % 3
                gt = s * T + tt * 128
                src = o_d[s].rearrange("(e p) t -> p e t", p=128)[
                    :, :, tt * 128:(tt + 1) * 128]
                O.op("sync", lambda h, o=oblk[s4][:], i_=src:
                     h.dma_start(out=o, in_=i_),
                     reads=R_od[s], writes=[R_oblk[s4]], dma=sOB[s4])
                O.op("vector", lambda h, o=oblkh[sl][:], a=oblk[s4][:]:
                     h.tensor_copy(o, a),
                     reads=[R_oblk[s4]], writes=[R_oblkh[sl]])
                O.op("vector", lambda h, o=oblkl[sl][:], a=oblk[s4][:],
                     b=oblkh[sl][:]: h.tensor_tensor(o, a, b, ALU.subtract),
                     reads=[R_oblk[s4], R_oblkh[sl]], writes=[R_oblkl[sl]])
                O.op("sync", lambda h, o=xw[sl][:], i_=x_in[gt:gt + 128, :]:
                     h.dma_start(out=o, in_=i_),
                     reads=[R_x_in], writes=[R_xw[sl]], dma=sXW[sl])
                b0 = 2 * sl
                for half, (c0, cw) in enumerate(EBLK):
                    ops = []
                    for e in range(EC):
                        ops.append((oblkh[sl], R_oblkh[sl], w2sh, R_w2sh, e))
                        ops.append((oblkh[sl], R_oblkh[sl], w2sl, R_w2sl, e))
                        ops.append((oblkl[sl], R_oblkl[sl], w2sh, R_w2sh, e))
                    for i, (ob, ro, wt, rw, e) in enumerate(ops):
                        spv = (i == len(ops) - 1)
                        O.op("tensor", lambda h, o=bank[b0 + half][:, 0:cw],
                             l=ob[:, e, :],
                             r=wt[:, e, c0:c0 + cw],
                             st=(i == 0), sp=(i == len(ops) - 1):
                             h.matmul(o, l, r, start=st, stop=sp),
                             reads=[ro, rw], writes=[Rb[b0 + half]], noinc=not spv)
                    O.op("vector", lambda h,
                         o=x1t[sl][:, c0:c0 + cw],
                         a=bank[b0 + half][:, 0:cw],
                         b=xw[sl][:, c0:c0 + cw]:
                         h.tensor_tensor(o, a, b, ALU.add),
                         reads=[Rb[b0 + half], R_xw[sl]], writes=[R_x1t[sl]])
                ti = s * TT + tt
                O.op("gpsimd", lambda h, o=x1_d[gt:gt + 128, :], i_=x1t[sl][:]:
                     h.dma_start(out=o, in_=i_),
                     reads=[R_x1t[sl]], writes=[R_x1d[ti]], dma=sX1[sl])
                emit_layernorm(x1t[sl][:], R_x1t[sl], yt[sl][:], R_yt[sl])
                O.op("vector", lambda h, a=yt[sl][:]: h.tensor_reduce(
                    c_bn[:, 14:15], a, AX.X, ALU.max, apply_absolute_value=True),
                    reads=[R_yt[sl]], writes=[R_bn])
                O.op("vector", lambda h: h.tensor_tensor(
                    c_amax[:, 0:1], c_amax[:, 0:1], c_bn[:, 14:15], ALU.max),
                    reads=[R_bn, R_amax1], writes=[R_amax1])
                if s + 1 < B:
                    emit_ln1_dve(s + 1, tt)
                    if tt >= 1:
                        emit_ln1_pe(s + 1, tt - 1)
                    if tt == TT - 1:
                        emit_ln1_pe(s + 1, TT - 1)
                while jobs and (njobs - len(jobs)) < ((tt + 1) * njobs) // TT:
                    jj = (njobs - len(jobs)) % 2
                    (srcw, rsrc, wcols, ncc, dstd, rdst, c,
                     scd, rscd) = jobs.pop(0)
                    O.op("sync", lambda h, a=pdr[jj][:, 0:wcols], srcw=srcw:
                         h.dma_start(out=a, in_=srcw),
                         reads=[rsrc], writes=[Rpdr[jj]], dma=sPL[jj])
                    emit_wquant_units(pdr[jj][:, 0:wcols], Rpdr[jj])
                    emit_units_cast(pdr[jj][:, 0:wcols], Rpdr[jj],
                                    pdrb[jj][:, 0:wcols], Rpdrb[jj])
                    emit_trans_store(pdrb[jj], Rpdrb[jj], pds, Rpds, ncc,
                                     dstd, rdst, c, sPS[jj], scd, rscd)
            O.barrier()
            es_w2.close()
        es_ln1.close()
        es_w2keep.close()

        sCC = O.new_sem("cc")
        sCL = O.new_sem("ccl")
        sCS = O.new_sem("ccs")

        def emit_sync_scale(which, amax_col, R_amax_c, clip_col, sc_buf, R_sc):
            O.op("tensor", lambda h, o=bank[0][0:1, 0:128],
                 i_=c_amax[:, amax_col:amax_col + 1]:
                 h.transpose(o, i_, c_ident[:]),
                 reads=[R_amax_c, R_ident], writes=[Rb[0]])
            O.op("vector", lambda h: h.tensor_reduce(
                c_scal[0:1, 0:1], bank[0][0:1, 0:128], AX.X, ALU.max),
                reads=[Rb[0]], writes=[R_scal])
            O.op("vector", lambda h: h.tensor_copy(
                c_scal[0:1, 0:8], c_scal[0:1, 0:1].broadcast_to((1, 8))),
                reads=[R_scal], writes=[R_scal])
            O.op("gpsimd", lambda h, o=cc_in_d[which]: h.dma_start(
                out=o, in_=c_scal[0:1, 0:8]),
                reads=[R_scal], writes=[R_ccin[which]], dma=sCS)
            O.op("gpsimd", lambda h, i_=cc_in_d[which], o=cc_out_d[which]:
                 h.collective_compute(
                     "AllReduce", ALU.max,
                     replica_groups=[list(range(cfg.n_cores))],
                     ins=[i_], outs=[o]),
                 reads=[R_ccin[which]], writes=[R_ccout[which]], sem=(sCC, 1))
            O.op("sync", lambda h, i_=cc_out_d[which]: h.dma_start(
                out=c_scal[0:1, 2:4], in_=i_[0:1, 0:2]),
                reads=[R_ccout[which]], writes=[R_scal], dma=sCL)
            O.op("scalar", lambda h: h.activation(
                c_scal[0:1, 4:5], c_clips[0:1, clip_col:clip_col + 1], AF.Exp,
                bias=0.0, scale=-1.0),
                reads=[R_clips], writes=[R_scal])
            O.op("vector", lambda h: h.tensor_scalar(
                c_scal[0:1, 4:5], c_scal[0:1, 4:5], 1.0, None, ALU.add),
                reads=[R_scal], writes=[R_scal])
            O.op("vector", lambda h: h.reciprocal(
                c_scal[0:1, 4:5], c_scal[0:1, 4:5]),
                reads=[R_scal], writes=[R_scal])
            O.op("vector", lambda h: h.tensor_tensor(
                c_scal[0:1, 4:5], c_scal[0:1, 4:5], c_scal[0:1, 2:3], ALU.mult),
                reads=[R_scal], writes=[R_scal])
            O.op("vector", lambda h: h.tensor_scalar(
                c_scal[0:1, 5:6], c_scal[0:1, 4:5], C127, None, ALU.mult),
                reads=[R_scal], writes=[R_scal])
            O.op("vector", lambda h: h.reciprocal(
                c_scal[0:1, 6:7], c_scal[0:1, 4:5]),
                reads=[R_scal], writes=[R_scal])
            O.op("vector", lambda h: h.tensor_scalar(
                c_scal[0:1, 6:7], c_scal[0:1, 6:7], 127.0, None, ALU.mult),
                reads=[R_scal], writes=[R_scal])
            O.op("tensor", lambda h: h.matmul(
                bank[1][:, 0:2], c_ones[:], c_scal[0:1, 5:7],
                start=True, stop=True),
                reads=[R_ones, R_scal], writes=[Rb[1]])
            O.op("vector", lambda h, o=sc_buf[:, 0:2]: h.tensor_copy(
                o, bank[1][:, 0:2]),
                reads=[Rb[1]], writes=[R_sc])

        emit_sync_scale(0, 0, R_amax1, 0, c_sc1, R_sc1)

        es2 = contextlib.ExitStack()
        xqf = es2.enter_context(
            nc.sbuf_tensor("xqf", [128, EC, TTC * 128], BF16))
        R_xqf = Reg("xqf")
        yin = [sb(es2, f"yin{i}", [128, E]) for i in range(2)]
        R_yin = [Reg("yin0"), Reg("yin1")]
        sYI = [O.new_sem("yi0"), O.new_sem("yi1")]
        yinb = [es2.enter_context(
            nc.sbuf_tensor(f"yinb{i}", [128, E], BF16)) for i in range(2)]
        R_yinb = [Reg("yinb0"), Reg("yinb1")]
        wqg = [es2.enter_context(
            nc.sbuf_tensor(f"wqg{i}", [128, EC, 128], BF16)) for i in range(2)]
        wqu = [es2.enter_context(
            nc.sbuf_tensor(f"wqu{i}", [128, EC, 128], BF16)) for i in range(2)]
        R_wqg = [Reg("wqg0"), Reg("wqg1")]
        R_wqu = [Reg("wqu0"), Reg("wqu1")]
        sWG = [O.new_sem("wg0"), O.new_sem("wg1")]
        sWU = [O.new_sem("wu0"), O.new_sem("wu1")]
        sgl = [sb(es2, f"sgl{i}", [128, 512]) for i in range(2)]
        R_sgl = [Reg("sgl0"), Reg("sgl1")]
        hpc = sb(es2, "hpc", [128, TTC * 128])
        R_hpc = Reg("hpc")
        sHP = [O.new_sem(f"hpst{i}") for i in range(4)]
        wgssb = sb(es2, "wgssb", [128, IC])
        wussb = sb(es2, "wussb", [128, IC])
        R_wgss, R_wuss = Reg("wgss"), Reg("wuss")
        c_gu = sb(es2, "c_gu", [128, 4])
        R_gu = [Reg("gu0"), Reg("gu1")]
        sWS = O.new_sem("wssl")

        wsraw = sb(es2, "wsraw", [IC, 128])
        R_wsraw = Reg("wsraw")
        for (srcd, rsrcd, dst, rdst) in ((wgs_d, R_wgsd, wgssb, R_wgss),
                                         (wus_d, R_wusd, wussb, R_wuss)):
            O.op("sync", lambda h, o=wsraw[:], i_=srcd: h.dma_start(out=o, in_=i_),
                 reads=[rsrcd], writes=[R_wsraw], dma=sWS)
            O.op("tensor", lambda h, i_=wsraw[:]: h.transpose(
                bank[6][:, 0:IC], i_, c_ident[0:IC, 0:IC]),
                reads=[R_wsraw, R_ident], writes=[Rb[6]])
            O.op("vector", lambda h, o=dst[:]: h.tensor_copy(o, bank[6][:, 0:IC]),
                 reads=[Rb[6]], writes=[rdst])

        for ti in range(TTC):
            sl = ti % 2
            gt = ti * 128
            O.op("sync", lambda h, o=yin[sl][:], i_=x1_d[gt:gt + 128, :]:
                 h.dma_start(out=o, in_=i_),
                 reads=[R_x1d[ti]], writes=[R_yin[sl]], dma=sYI[sl])
            emit_layernorm(yin[sl][:], R_yin[sl], yin[sl][:], R_yin[sl])
            a = yin[sl][:]
            ab = yinb[sl][:]
            O.op("vector", lambda h, a=a: h.tensor_scalar(
                a.bitcast(I32), a, c_sc1[:, 1:2], None, ALU.mult),
                reads=[R_yin[sl], R_sc1], writes=[R_yin[sl]])
            emit_units_cast(a, R_yin[sl], ab, R_yinb[sl])
            for half in range((EC + 3) // 4):
                n4 = min(4, EC - half * 4)
                bk = 4 + half % 2
                bb = bank[bk].ap().bitcast(BF16)
                for k in range(n4):
                    e = half * 4 + k
                    O.op("tensor", lambda h, i_=yinb[sl][:, e * 128:(e + 1) * 128],
                         o=bb[:, k * 128:(k + 1) * 128]:
                         h.transpose(o, i_, c_identb[:]),
                         reads=[R_yinb[sl], R_identb], writes=[Rb[bk]])
                dst = xqf[:, half * 4:half * 4 + n4, gt:gt + 128]
                O.op("scalar", lambda h, o=dst, i_=bb[:, 0:n4 * 128]
                     .rearrange("p (e m) -> p e m", m=128): h.copy(o, i_),
                     reads=[Rb[bk]], writes=[R_xqf])

        NBLK = TTC * 128 // 512
        for c in range(IC):
            sl = c % 2
            O.op("sync", lambda h, o=wqg[sl][:],
                 i_=wgq_d[:, c * 128:(c + 1) * 128]
                 .rearrange("(e p) m -> p e m", p=128):
                 h.dma_start(out=o, in_=i_),
                 reads=[R_wgqd[c]], writes=[R_wqg[sl]], dma=sWG[sl])
            O.op("sync", lambda h, o=wqu[sl][:],
                 i_=wuq_d[:, c * 128:(c + 1) * 128]
                 .rearrange("(e p) m -> p e m", p=128):
                 h.dma_start(out=o, in_=i_),
                 reads=[R_wuqd[c]], writes=[R_wqu[sl]], dma=sWU[sl])
            O.op("vector", lambda h, o=c_gu[:, 2 * sl:2 * sl + 1],
                 a=wgssb[:, c:c + 1]: h.tensor_tensor(
                     o, a, c_sc1[:, 0:1], ALU.mult),
                 reads=[R_wgss, R_sc1], writes=[R_gu[sl]])
            O.op("vector", lambda h, o=c_gu[:, 2 * sl + 1:2 * sl + 2],
                 a=wussb[:, c:c + 1]: h.tensor_tensor(
                     o, a, c_sc1[:, 0:1], ALU.mult),
                 reads=[R_wuss, R_sc1], writes=[R_gu[sl]])
            for b in range(NBLK):
                bg = b % 2
                bu = 2 + b % 2
                for e in range(EC):
                    spv = (e == EC - 1)
                    O.op("tensor", lambda h, o=bank[bg][:, 0:512],
                         l=wqg[sl][:, e, :], r=xqf[:, e, b * 512:(b + 1) * 512],
                         st=(e == 0), sp=(e == EC - 1):
                         h.matmul(o, l, r, start=st, stop=sp),
                         reads=[R_wqg[sl], R_xqf], writes=[Rb[bg]], noinc=not spv)
                for e in range(EC):
                    spv = (e == EC - 1)
                    O.op("tensor", lambda h, o=bank[bu][:, 0:512],
                         l=wqu[sl][:, e, :], r=xqf[:, e, b * 512:(b + 1) * 512],
                         st=(e == 0), sp=(e == EC - 1):
                         h.matmul(o, l, r, start=st, stop=sp),
                         reads=[R_wqu[sl], R_xqf], writes=[Rb[bu]], noinc=not spv)
                O.op("scalar", lambda h, o=sgl[b % 2][:], a=bank[bg][:, 0:512],
                     sc=c_gu[:, 2 * sl:2 * sl + 1]:
                     h.activation(o, a, AF.Silu, bias=0.0, scale=sc),
                     reads=[Rb[bg], R_gu[sl]], writes=[R_sgl[b % 2]])
                O.op("vector", lambda h, o=hpc[:, b * 512:(b + 1) * 512],
                     a=bank[bu][:, 0:512], sg=sgl[b % 2][:],
                     sc=c_gu[:, 2 * sl + 1:2 * sl + 2]:
                     h.scalar_tensor_tensor(o, a, sc, sg, ALU.mult, ALU.mult),
                     reads=[Rb[bu], R_gu[sl], R_sgl[b % 2]], writes=[R_hpc])
                O.op("vector", lambda h, a=hpc[:, b * 512:(b + 1) * 512]:
                     h.tensor_reduce(c_bn[:, 14:15], a, AX.X, ALU.max,
                                     apply_absolute_value=True),
                     reads=[R_hpc], writes=[R_bn])
                O.op("vector", lambda h: h.tensor_tensor(
                    c_amax[:, 1:2], c_amax[:, 1:2], c_bn[:, 14:15], ALU.max),
                    reads=[R_bn, R_amax2], writes=[R_amax2])
            O.op("gpsimd", lambda h, o=hp_d[c], i_=hpc[:]:
                 h.dma_start(out=o, in_=i_),
                 reads=[R_hpc], writes=[R_hpd[c]], dma=sHP[c % 4])

        O.barrier()
        es2.close()

        es3 = contextlib.ExitStack()
        wqd = es3.enter_context(
            nc.sbuf_tensor("wqd", [128, IC, E], BF16))
        R_wqd = Reg("wqd")
        sWD = O.new_sem("wqd")
        HW3 = 512
        hps = [sb(es3, f"hps{i}", [128, HW3]) for i in range(4)]
        R_hps = [Reg(f"hps{i}") for i in range(4)]
        sHT = [O.new_sem(f"ht{i}") for i in range(8)]
        hptb = [es3.enter_context(
            nc.sbuf_tensor(f"hptb{i}", [128, IC, HW3], BF16)) for i in range(2)]
        R_hptb = [Reg("hptb0"), Reg("hptb1")]
        dwn = [sb(es3, f"dwn{i}", [128, EC, HW3]) for i in range(2)]
        R_dwn = [Reg("dwn0"), Reg("dwn1")]
        x1in = [sb(es3, f"x1in{i}", [128, E]) for i in range(4)]
        R_x1in = [Reg(f"x1i{i}") for i in range(4)]
        sX1I = [O.new_sem(f"x1i{i}") for i in range(4)]
        outt = [sb(es3, f"outt{i}", [128, E]) for i in range(4)]
        R_outt = [Reg(f"otr{i}") for i in range(4)]
        sOT = [O.new_sem(f"ot{i}") for i in range(4)]
        wdssb = sb(es3, "wdssb", [128, EC])
        R_wdss = Reg("wdss")
        c_ds = sb(es3, "c_ds", [128, EC])
        R_ds = Reg("cds")
        wsraw3 = sb(es3, "wsraw3", [EC, 128])
        R_wsraw3 = Reg("wsraw3")

        O.op("sync", lambda h, o=wqd[:],
             i_=wdq_d.rearrange("(c p) e -> p c e", p=128):
             h.dma_start(out=o, in_=i_),
             reads=R_wdqd, writes=[R_wqd], dma=sWD)
        O.op("sync", lambda h, o=wsraw3[:], i_=wds_d: h.dma_start(out=o, in_=i_),
             reads=[R_wdsd], writes=[R_wsraw3], dma=sWD)
        emit_sync_scale(1, 1, R_amax2, 1, c_sc2, R_sc2)
        O.op("tensor", lambda h: h.transpose(
            bank[6][:, 0:EC], wsraw3[:], c_ident[0:EC, 0:EC]),
            reads=[R_wsraw3, R_ident], writes=[Rb[6]])
        O.op("vector", lambda h: h.tensor_copy(wdssb[:], bank[6][:, 0:EC]),
             reads=[Rb[6]], writes=[R_wdss])
        O.op("vector", lambda h: h.tensor_scalar(
            c_ds[:], wdssb[:], c_sc2[:, 0:1], None, ALU.mult),
            reads=[R_wdss, R_sc2], writes=[R_ds])

        NHG = TTC * 128 // HW3
        for hg in range(NHG):
            sl = hg % 2
            for c in range(IC):
                cs = c % 4
                O.op("sync", lambda h, o=hps[cs][:],
                     i_=hp_d[c][:, hg * HW3:(hg + 1) * HW3]:
                     h.dma_start(out=o, in_=i_),
                     reads=[R_hpd[c]], writes=[R_hps[cs]], dma=sHT[c % 8])
                O.op("vector", lambda h, a=hps[cs][:]: h.tensor_scalar(
                    a.bitcast(I32), a, c_sc2[:, 1:2], None, ALU.mult),
                    reads=[R_hps[cs], R_sc2], writes=[R_hps[cs]])
                emit_units_cast(hps[cs][:], R_hps[cs],
                                hptb[sl][:, c, :], R_hptb[sl])
            for oc in range(EC):
                bk = oc % 4
                for c in range(IC):
                    spv = (c == IC - 1)
                    O.op("tensor", lambda h, o=bank[bk][:, 0:HW3],
                         l=wqd[:, c, oc * 128:(oc + 1) * 128],
                         r=hptb[sl][:, c, :],
                         st=(c == 0), sp=(c == IC - 1):
                         h.matmul(o, l, r, start=st, stop=sp),
                         reads=[R_hptb[sl], R_wqd], writes=[Rb[bk]], noinc=not spv)
                O.op("vector", lambda h, o=dwn[sl][:, oc, :], a=bank[bk][:, 0:HW3],
                     sc=c_ds[:, oc:oc + 1]: h.tensor_scalar(
                         o, a, sc, None, ALU.mult),
                     reads=[Rb[bk], R_ds], writes=[R_dwn[sl]])
            for t4 in range(HW3 // 128):
                ti = hg * (HW3 // 128) + t4
                gt = ti * 128
                xsl = ti % 4
                O.op("sync", lambda h, o=x1in[xsl][:], i_=x1_d[gt:gt + 128, :]:
                     h.dma_start(out=o, in_=i_),
                     reads=[R_x1d[ti]], writes=[R_x1in[xsl]], dma=sX1I[xsl])
                for half in range((EC + 3) // 4):
                    n4 = min(4, EC - half * 4)
                    bk = 4 + half % 2
                    for k in range(n4):
                        oc = half * 4 + k
                        O.op("tensor", lambda h,
                             i_=dwn[sl][:, oc, t4 * 128:(t4 + 1) * 128],
                             o=bank[bk][:, k * 128:(k + 1) * 128]:
                             h.transpose(o, i_, c_ident[:]),
                             reads=[R_dwn[sl], R_ident], writes=[Rb[bk]])
                    O.op("vector", lambda h,
                         o=outt[xsl][:, half * 512:half * 512 + n4 * 128],
                         a=bank[bk][:, 0:n4 * 128],
                         b=x1in[xsl][:, half * 512:half * 512 + n4 * 128]:
                         h.tensor_tensor(o, a, b, ALU.add),
                         reads=[Rb[bk], R_x1in[xsl]], writes=[R_outt[xsl]])
                O.op("gpsimd", lambda h, o=out_d[gt:gt + 128, :],
                     i_=outt[xsl][:]: h.dma_start(out=o, in_=i_),
                     reads=[R_outt[xsl]], writes=[R_out], dma=sOT[xsl])
        es3.close()

        O.final_wait("gpsimd", sOT)

        with nc.Block() as block:
            O.replay(block)
    return nc


def _host_constants(cfg):
    T, D = cfg.T, cfg.D
    inv = 1.0 / (10000.0 ** (np.arange(0, D, 2, dtype=np.float32) / D))
    ang = np.arange(T, dtype=np.float32)[:, None] * inv[None, :]
    cos = np.concatenate([np.cos(ang), np.cos(ang)], axis=1).astype(np.float32)
    sin = np.concatenate([np.sin(ang), np.sin(ang)], axis=1).astype(np.float32)
    sinf = sin.copy()
    sinf[:, : D // 2] *= -1.0
    ident = np.eye(128, dtype=np.float32)
    dmask = np.tril(np.full((128, 128), MASKVAL, dtype=np.float32), k=-1)
    return cos, sinf, ident, dmask


def _host_w1_reorder(W1, cfg):
    E = cfg.E
    perm = []
    for g in range(cfg.G):
        for hh in (2 * g, 2 * g + 1):
            perm.extend(range(hh * 64, hh * 64 + 64))
            perm.extend(range(E + hh * 64, E + hh * 64 + 64))
        for hh in (2 * g, 2 * g + 1):
            perm.extend(range(2 * E + hh * 64, 2 * E + hh * 64 + 64))
    return np.ascontiguousarray(W1[np.array(perm), :].T)


def make_in_maps(cfg, X, W1, W2, wg, wu, wd, clip_x, clip_down):
    cos, sinf, ident, dmask = _host_constants(cfg)
    w1t = _host_w1_reorder(np.asarray(W1, np.float32), cfg)
    w2t = np.ascontiguousarray(np.asarray(W2, np.float32).T)
    clips = np.array([[float(clip_x), float(clip_down)]], np.float32)
    in_maps = []
    for c in range(cfg.n_cores):
        xs = np.ascontiguousarray(
            X[c * cfg.B_loc:(c + 1) * cfg.B_loc].reshape(
                cfg.B_loc * cfg.T, cfg.E))
        in_maps.append({
            "x": xs,
            "w1t": w1t, "w2t": w2t,
            "wg": np.ascontiguousarray(np.asarray(wg, np.float32)),
            "wu": np.ascontiguousarray(np.asarray(wu, np.float32)),
            "wd": np.ascontiguousarray(np.asarray(wd, np.float32)),
            "clips": clips, "cost": cos, "sinft": sinf,
            "ident": ident, "dmask": dmask,
        })
    return in_maps


def kernel(L0_attn_x, L0_attn_W1, L0_attn_W2, L0_ffn_clip_x, L0_ffn_clip_down,
           L0_ffn_w_gate, L0_ffn_w_up, L0_ffn_w_down):
    cfg = Cfg()
    X = np.ascontiguousarray(np.asarray(L0_attn_x, dtype=np.float32))
    nc = build(cfg)
    in_maps = make_in_maps(cfg, X, L0_attn_W1, L0_attn_W2,
                           L0_ffn_w_gate, L0_ffn_w_up, L0_ffn_w_down,
                           np.asarray(L0_ffn_clip_x)[0],
                           np.asarray(L0_ffn_clip_down)[0])
    res = run_bass_kernel_spmd(nc, in_maps, core_ids=list(range(cfg.n_cores)))
    outs = [r["out"].reshape(cfg.B_loc, cfg.T, cfg.E) for r in res.results]
    return np.concatenate(outs, axis=0)


# version 54
# speedup vs baseline: 1.0303x; 1.0303x over previous
import contextlib
import numpy as np

import concourse.bass as bass
import concourse.mybir as mybir
from concourse.bass_utils import run_bass_kernel_spmd

F32 = mybir.dt.float32
AF = mybir.ActivationFunctionType
ALU = mybir.AluOpType
AX = mybir.AxisListType

ENGS = ("sync", "tensor", "vector", "scalar", "gpsimd")
DRAIN_ENGS = ("vector", "scalar", "gpsimd")


class Reg:

    __slots__ = ("name", "lw", "readers")

    def __init__(self, name):
        self.name = name
        self.lw = None
        self.readers = {}


class Ops:

    def __init__(self, nc, es):
        self.nc = nc
        self.es = es
        self.q = {e: [] for e in ENGS}
        self.semh = {}
        self.semv = {}
        self.waited = {e: {} for e in ENGS}
        self.dirty = {e: (set(), set()) for e in ENGS}
        self.esem = {}
        for e in ("tensor", "vector", "scalar", "gpsimd"):
            self.esem[e] = self.new_sem("se_" + e)

    def new_sem(self, name):
        h = self.es.enter_context(self.nc.semaphore(name))
        self.semh[name] = h
        self.semv[name] = 0
        return name

    def _wait(self, eng, semname, val, waits):
        if val <= 0:
            return
        if self.waited[eng].get(semname, 0) >= val:
            return
        self.waited[eng][semname] = val
        waits.append((self.semh[semname], val))

    def op(self, eng, builder, reads=(), writes=(), dma=None, sem=None,
           noinc=False):
        waits = []
        drain = False
        wr, rd = self.dirty.get(eng, (set(), set()))
        for r in reads:
            if r.lw is not None:
                peng, sname, val = r.lw
                if peng == eng:
                    if id(r) in wr:
                        drain = True
                else:
                    self._wait(eng, sname, val, waits)
        for w in writes:
            if w.lw is not None:
                peng, sname, val = w.lw
                if peng == eng:
                    if id(w) in wr:
                        drain = True
                else:
                    self._wait(eng, sname, val, waits)
            for key, (sname, val) in w.readers.items():
                if key == eng:
                    if id(w) in rd or id(w) in wr:
                        drain = True
                else:
                    self._wait(eng, sname, val, waits)
        if drain and eng in DRAIN_ENGS:
            wr.clear()
            rd.clear()
        else:
            drain = False
        if dma is not None:
            sem = (dma, 16)
        if sem is not None:
            sname, incval = sem
            self.semv[sname] += incval
            val = self.semv[sname]
            prodkey = sname
        elif noinc and eng != "sync":
            sname = "se_" + eng
            val, incval = self.semv[sname] + 1, 0
            prodkey = eng
        else:
            sname = "se_" + eng if eng != "sync" else None
            if sname is not None:
                self.semv[sname] += 1
                val, incval = self.semv[sname], 1
            else:
                val, incval = 0, 0
            prodkey = eng
        for w in writes:
            w.lw = ((prodkey if sem is None else None), sname, val)
            w.readers = {}
        for r in reads:
            key = sname if sem is not None else eng
            r.readers[key] = (sname, val)
        if eng in DRAIN_ENGS:
            for w in writes:
                wr.add(id(w))
            for r in reads:
                rd.add(id(r))
        semh = (self.semh.get(sname)
                if sname is not None and incval > 0 else None)
        self.q[eng].append((drain, waits, builder, semh, incval))
        if sname is not None and sem is None and incval > 0:
            self.waited[eng][sname] = val

    def barrier(self):
        targets = [(n, v) for n, v in self.semv.items() if v > 0]
        for e in ENGS:
            waits = []
            for n, v in targets:
                if n == "se_" + e:
                    continue
                self._wait(e, n, v, waits)
            dr = e in DRAIN_ENGS
            if waits or dr:
                self.q[e].append((dr, waits, None, None, 0))
            if dr:
                self.dirty[e][0].clear()
                self.dirty[e][1].clear()

    def final_wait(self, eng, semnames):
        waits = []
        for n in semnames:
            self._wait(eng, n, self.semv[n], waits)
        if waits:
            self.q[eng].append((False, waits, None, None, 0))

    def replay(self, block):
        def run(eng):
            def body(h):
                for drain, waits, builder, semh, incval in self.q[eng]:
                    if drain:
                        h.drain()
                    for sh, v in waits:
                        h.wait_ge(sh, v)
                    if builder is not None:
                        inst = builder(h)
                        if semh is not None:
                            inst.then_inc(semh, incval)
            return body

        block.sync(run("sync"))
        block.tensor(run("tensor"))
        block.vector(run("vector"))
        block.scalar(run("scalar"))
        block.gpsimd(run("gpsimd"))


class Cfg:
    def __init__(self, B_loc=2, T=2048, E=1024, H=16, I=2816, n_cores=8):
        self.B_loc, self.T, self.E, self.H, self.I = B_loc, T, E, H, I
        self.n_cores = n_cores
        self.D = 64
        self.EC = E // 128
        self.IC = I // 128
        self.TT = T // 128
        self.TTC = B_loc * T // 128
        self.G = H // 2


C127 = 1.0 / 127.0
MASKVAL = -1.0e30
I32 = mybir.dt.int32
BF16 = mybir.dt.bfloat16


def build(cfg):
    nc = bass.Bass("TRN2", target_bir_lowering=False)
    B, T, E, H, I, D = cfg.B_loc, cfg.T, cfg.E, cfg.H, cfg.I, cfg.D
    EC, IC, TT, TTC, G = cfg.EC, cfg.IC, cfg.TT, cfg.TTC, cfg.G
    NT = B * T
    EBLK = [(i * 512, min(512, E - i * 512)) for i in range((E + 511) // 512)]

    x_in = nc.dram_tensor("x", [NT, E], F32, kind="ExternalInput").ap()
    w1t = nc.dram_tensor("w1t", [E, 3 * E], F32, kind="ExternalInput").ap()
    w2t = nc.dram_tensor("w2t", [E, E], F32, kind="ExternalInput").ap()
    wg_in = nc.dram_tensor("wg", [I, E], F32, kind="ExternalInput").ap()
    wu_in = nc.dram_tensor("wu", [I, E], F32, kind="ExternalInput").ap()
    wd_in = nc.dram_tensor("wd", [E, I], F32, kind="ExternalInput").ap()
    clips = nc.dram_tensor("clips", [1, 2], F32, kind="ExternalInput").ap()
    cost_in = nc.dram_tensor("cost", [T, D], F32, kind="ExternalInput").ap()
    sinft_in = nc.dram_tensor("sinft", [T, D], F32, kind="ExternalInput").ap()
    ident_in = nc.dram_tensor("ident", [128, 128], F32, kind="ExternalInput").ap()
    dmask_in = nc.dram_tensor("dmask", [128, 128], F32, kind="ExternalInput").ap()
    out_d = nc.dram_tensor("out", [NT, E], F32, kind="ExternalOutput").ap()

    x1_d = nc.dram_tensor("x1_d", [NT, E], F32).ap()
    o_d = nc.dram_tensor("o_d", [B, E, T], F32).ap()
    hp_d = nc.dram_tensor("hp_d", [IC, 128, NT], F32).ap()
    wgq_d = nc.dram_tensor("wgq_d", [E, I], BF16).ap()
    wuq_d = nc.dram_tensor("wuq_d", [E, I], BF16).ap()
    wdq_d = nc.dram_tensor("wdq_d", [I, E], BF16).ap()
    wgs_d = nc.dram_tensor("wgs_d", [IC, 128], F32).ap()
    wus_d = nc.dram_tensor("wus_d", [IC, 128], F32).ap()
    wds_d = nc.dram_tensor("wds_d", [EC, 128], F32).ap()
    cc_in_d = [nc.dram_tensor(f"cc_in{i}", [1, 8], F32).ap() for i in range(2)]
    cc_out_d = [nc.dram_tensor(f"cc_out{i}", [1, 8], F32).ap() for i in range(2)]

    es = contextlib.ExitStack()
    with es:
        O = Ops(nc, es)

        def sb(stack, name, shape):
            return stack.enter_context(nc.sbuf_tensor(name, shape, F32))

        c_ident = sb(es, "c_ident", [128, 128])
        c_identb = es.enter_context(nc.sbuf_tensor("c_identb", [128, 128], BF16))
        c_dmaskb = es.enter_context(nc.sbuf_tensor("c_dmaskb", [128, 128], BF16))
        c_cosT = sb(es, "c_cosT", [128, TT, D])
        c_sinfT = sb(es, "c_sinfT", [128, TT, D])
        c_ones = sb(es, "c_ones", [1, 128])
        c_clips = sb(es, "c_clips", [1, 2])
        c_scal = sb(es, "c_scal", [1, 8])
        c_amax = sb(es, "c_amax", [128, 2])
        c_sc1 = sb(es, "c_sc1", [128, 4])
        c_sc2 = sb(es, "c_sc2", [128, 4])
        c_bn = sb(es, "c_bn", [128, 16])
        c_wsc = sb(es, "c_wsc", [128, 2])

        bank = [es.enter_context(nc.psum_tensor(f"pb{i}", [128, 512], F32))
                for i in range(8)]
        Rb = [Reg(f"pb{i}") for i in range(8)]

        R_ident, R_dmask = Reg("ident"), Reg("dmask")
        R_identb = Reg("identb")
        R_dmaskb = Reg("dmaskb")
        R_cosT, R_sinfT = Reg("cosT"), Reg("sinfT")
        R_ones, R_clips, R_scal = Reg("ones"), Reg("clips"), Reg("scal")
        R_amax1, R_amax2 = Reg("amax1"), Reg("amax2")
        R_sc1, R_sc2 = Reg("sc1"), Reg("sc2")
        R_bn, R_row, R_wsc = Reg("bn"), Reg("row"), Reg("wsc")
        R_x_in, R_w1t, R_w2t = Reg("x_in"), Reg("w1t"), Reg("w2t")
        R_wg_in, R_wu_in, R_wd_in = Reg("wg_in"), Reg("wu_in"), Reg("wd_in")
        R_out = Reg("out")
        R_ccin = [Reg("ccin0"), Reg("ccin1")]
        R_ccout = [Reg("ccout0"), Reg("ccout1")]
        R_x1d = [Reg(f"x1d{t}") for t in range(TTC)]
        R_od = [[Reg(f"od{s}_{h}") for h in range(H)] for s in range(B)]
        R_hpd = [Reg(f"hpd{c}") for c in range(IC)]
        R_wgqd = [Reg(f"wgqd{c}") for c in range(IC)]
        R_wuqd = [Reg(f"wuqd{c}") for c in range(IC)]
        R_wdqd = [Reg(f"wdqd{c}") for c in range(EC)]
        R_wgsd, R_wusd, R_wdsd = Reg("wgsd"), Reg("wusd"), Reg("wdsd")

        sC = O.new_sem("ldc")
        O.op("sync", lambda h: h.dma_start(out=c_ident[:], in_=ident_in),
             writes=[R_ident], dma=sC)
        O.op("sync", lambda h: h.dma_start(
            out=c_cosT[:, 0:2, :],
            in_=dmask_in.rearrange("p (a b) -> p a b", b=D)),
            writes=[R_dmask], dma=sC)
        O.op("sync", lambda h: h.dma_start(
            out=c_sinfT[:], in_=sinft_in.rearrange("(tt p) d -> p tt d", p=128)),
            writes=[R_sinfT], dma=sC)
        O.op("sync", lambda h: h.dma_start(out=c_clips[:], in_=clips),
             writes=[R_clips], dma=sC)
        O.op("gpsimd", lambda h: h.memset(c_ones[:], 1.0), writes=[R_ones])
        O.op("gpsimd", lambda h: h.memset(c_amax[:], 0.0),
             writes=[R_amax1, R_amax2])
        for r in (R_ident, R_dmask, R_cosT, R_sinfT, R_clips):
            r.lw = (None, sC, O.semv[sC])
        O.op("vector", lambda h: h.tensor_copy(c_identb[:], c_ident[:]),
             reads=[R_ident], writes=[R_identb])
        O.op("vector", lambda h: h.tensor_copy(
            c_dmaskb[:], c_cosT[:, 0:2, :].rearrange("p a b -> p (a b)")),
            reads=[R_dmask], writes=[R_dmaskb])
        O.op("sync", lambda h: h.dma_start(
            out=c_cosT[:], in_=cost_in.rearrange("(tt p) d -> p tt d", p=128)),
            writes=[R_cosT, R_dmask], dma=sC)
        R_cosT.lw = (None, sC, O.semv[sC])

        sPL = [O.new_sem(f"pl{i}") for i in range(4)]
        sPS = [O.new_sem(f"psm{i}") for i in range(2)]
        sPSC = O.new_sem("psc0")

        def emit_units_cast(ap, Rr, apb, Rrb):
            O.op("vector", lambda h, a=ap, b=apb: h.tensor_copy(b, a.bitcast(I32)),
                 reads=[Rr], writes=[Rrb])
            O.op("vector", lambda h, b=apb: h.tensor_scalar(
                b, b, 127.0, -128.0, ALU.min, ALU.max),
                reads=[Rrb], writes=[Rrb])

        def emit_round_clip(ap, Rr):
            O.op("vector", lambda h, a=ap: h.tensor_copy(a.bitcast(I32), a),
                 reads=[Rr], writes=[Rr])
            O.op("vector", lambda h, a=ap: h.tensor_copy(a, a.bitcast(I32)),
                 reads=[Rr], writes=[Rr])
            O.op("vector", lambda h, a=ap: h.tensor_scalar(
                a, a, 127.0, -128.0, ALU.min, ALU.max),
                reads=[Rr], writes=[Rr])

        def emit_wquant_units(ap, Rr):
            O.op("vector", lambda h, a=ap: h.tensor_reduce(
                c_wsc[:, 0:1], a, AX.X, ALU.max, apply_absolute_value=True),
                reads=[Rr], writes=[R_wsc])
            O.op("vector", lambda h: h.reciprocal(c_wsc[:, 1:2], c_wsc[:, 0:1]),
                 reads=[R_wsc], writes=[R_wsc])
            O.op("vector", lambda h: h.tensor_scalar(
                c_wsc[:, 1:2], c_wsc[:, 1:2], 127.0, None, ALU.mult),
                reads=[R_wsc], writes=[R_wsc])
            O.op("vector", lambda h: h.tensor_scalar(
                c_wsc[:, 0:1], c_wsc[:, 0:1], C127, None, ALU.mult),
                reads=[R_wsc], writes=[R_wsc])
            O.op("vector", lambda h, a=ap: h.tensor_scalar(
                a.bitcast(I32), a, c_wsc[:, 1:2], None, ALU.mult),
                reads=[Rr, R_wsc], writes=[Rr])

        def emit_trans_store(rawb, Rrawb, stg, Rstg, ncc, dstd, rdst,
                             c, ssem, scd, rscd):
            b7 = bank[7].ap().bitcast(BF16)
            for e0 in range(0, ncc, 4):
                n4 = min(4, ncc - e0)
                for k in range(n4):
                    e = e0 + k
                    O.op("tensor", lambda h, i_=rawb[:, e * 128:(e + 1) * 128],
                         o=b7[:, k * 128:(k + 1) * 128]:
                         h.transpose(o, i_, c_identb[:]),
                         reads=[Rrawb, R_identb], writes=[Rb[7]])
                O.op("scalar", lambda h, o=stg[:, e0 * 128:(e0 + n4) * 128],
                     i_=b7[:, 0:n4 * 128]: h.copy(o, i_),
                     reads=[Rb[7]], writes=[Rstg])
            dst = dstd[:, c * 128:(c + 1) * 128].rearrange(
                "(e p) m -> p e m", p=128)
            O.op("gpsimd", lambda h, o=dst,
                 i_=stg[:, 0:ncc * 128].rearrange("p (e m) -> p e m", m=128):
                 h.dma_start(out=o, in_=i_),
                 reads=[Rstg], writes=[rdst], dma=ssem)
            O.op("gpsimd", lambda h, o=scd[c:c + 1, :].rearrange("a p -> p a"),
                 i_=c_wsc[:, 0:1]: h.dma_start(out=o, in_=i_),
                 reads=[R_wsc], writes=[rscd], dma=sPSC)

        prep_jobs = []
        for c in range(IC):
            prep_jobs.append((wg_in[c * 128:(c + 1) * 128, :], R_wg_in, E, EC,
                              wgq_d, R_wgqd[c], c, wgs_d, R_wgsd))
            prep_jobs.append((wu_in[c * 128:(c + 1) * 128, :], R_wu_in, E, EC,
                              wuq_d, R_wuqd[c], c, wus_d, R_wusd))
        for c in range(EC):
            prep_jobs.append((wd_in[c * 128:(c + 1) * 128, :], R_wd_in, I, IC,
                              wdq_d, R_wdqd[c], c, wds_d, R_wdsd))

        def emit_layernorm(x_ap, Rx, y_ap, Ry):
            nhalf = max(1, E // 512)
            w = min(512, E)
            for i in range(nhalf):
                O.op("vector", lambda h, o=c_bn[:, 6 * i:6 * i + 6],
                     a=x_ap[:, i * w:(i + 1) * w]: h.bn_stats(o, a),
                     reads=[Rx], writes=[R_bn])
            O.op("vector", lambda h, n=nhalf: h.bn_aggr(
                c_bn[:, 12:14], c_bn[:, 0:6 * n]),
                reads=[R_bn], writes=[R_bn])
            O.op("vector", lambda h: h.tensor_scalar(
                c_bn[:, 14:15], c_bn[:, 12:13], -1.0, None, ALU.mult),
                reads=[R_bn], writes=[R_bn])
            O.op("vector", lambda h: h.tensor_scalar(
                c_bn[:, 15:16], c_bn[:, 13:14], 1e-5, None, ALU.add),
                reads=[R_bn], writes=[R_bn])
            O.op("scalar", lambda h: h.activation(
                c_bn[:, 15:16], c_bn[:, 15:16], AF.Sqrt),
                reads=[R_bn], writes=[R_bn])
            O.op("vector", lambda h: h.reciprocal(c_bn[:, 15:16], c_bn[:, 15:16]),
                 reads=[R_bn], writes=[R_bn])
            O.op("vector", lambda h, o=y_ap, a=x_ap: h.tensor_scalar(
                o, a, c_bn[:, 14:15], c_bn[:, 15:16], ALU.add, ALU.mult),
                reads=[Rx, R_bn], writes=[Ry])

        es_w2keep = contextlib.ExitStack()
        w2sh = es_w2keep.enter_context(nc.sbuf_tensor("w2sh", [128, EC, E], BF16))
        w2sl = es_w2keep.enter_context(nc.sbuf_tensor("w2sl", [128, EC, E], BF16))
        R_w2sh, R_w2sl = Reg("w2sh"), Reg("w2sl")
        es_w2p = contextlib.ExitStack()
        w2stg = [sb(es_w2p, f"w2stg{i}", [128, 1, E]) for i in range(2)]
        R_w2stg = [Reg("w2stg0"), Reg("w2stg1")]
        sW2 = [O.new_sem("w2s0"), O.new_sem("w2s1")]
        for e0 in range(EC):
            sl = e0 % 2
            src = w2t[e0 * 128:(e0 + 1) * 128, :].rearrange(
                "(e p) m -> p e m", p=128)
            O.op("sync", lambda h, o=w2stg[sl][:], i_=src:
                 h.dma_start(out=o, in_=i_),
                 reads=[R_w2t], writes=[R_w2stg[sl]], dma=sW2[sl])
            O.op("vector", lambda h, o=w2sh[:, e0:e0 + 1, :],
                 a=w2stg[sl][:]: h.tensor_copy(o, a),
                 reads=[R_w2stg[sl]], writes=[R_w2sh])
            O.op("vector", lambda h, o=w2sl[:, e0:e0 + 1, :],
                 a=w2stg[sl][:], b=w2sh[:, e0:e0 + 1, :]:
                 h.tensor_tensor(o, a, b, ALU.subtract),
                 reads=[R_w2stg[sl], R_w2sh], writes=[R_w2sl])
        O.barrier()
        es_w2p.close()

        es_ln1 = contextlib.ExitStack()
        ln1h = es_ln1.enter_context(nc.sbuf_tensor("ln1h", [128, EC, T], BF16))
        ln1l = es_ln1.enter_context(nc.sbuf_tensor("ln1l", [128, EC, T], BF16))
        R_ln1h, R_ln1l = Reg("ln1h"), Reg("ln1l")
        xa = [sb(es_ln1, f"xa{i}", [128, E]) for i in range(3)]
        R_xa = [Reg(f"xa{i}") for i in range(3)]
        sXA = [O.new_sem(f"xa{i}") for i in range(3)]
        xab = [es_ln1.enter_context(
            nc.sbuf_tensor(f"xab{i}", [128, E], BF16)) for i in range(2)]
        R_xab = [Reg(f"xab{i}") for i in range(2)]

        def emit_ln1_dve(s2, tt):
            sl = tt % 3
            sl2 = tt % 2
            gt = s2 * T + tt * 128
            O.op("sync", lambda h, o=xa[sl][:], i_=x_in[gt:gt + 128, :]:
                 h.dma_start(out=o, in_=i_),
                 reads=[R_x_in], writes=[R_xa[sl]], dma=sXA[sl])
            emit_layernorm(xa[sl][:], R_xa[sl], xa[sl][:], R_xa[sl])
            O.op("vector", lambda h, o=xab[sl2][:], a=xa[sl][:]:
                 h.tensor_copy(o, a),
                 reads=[R_xa[sl]], writes=[R_xab[sl2]])
            O.op("vector", lambda h, o=xa[sl][:], a=xa[sl][:], b=xab[sl2][:]:
                 h.tensor_tensor(o, a, b, ALU.subtract),
                 reads=[R_xa[sl], R_xab[sl2]], writes=[R_xa[sl]])

        def emit_ln1_pe(s2, tt):
            sl = tt % 3
            sl2 = tt % 2
            for (srcb, rsrc, dstt, rdst) in (
                    (xab[sl2], R_xab[sl2], ln1h, R_ln1h),
                    (None, R_xa[sl], ln1l, R_ln1l)):
                for half in range((EC + 3) // 4):
                    n4 = min(4, EC - half * 4)
                    bk = 4 + half % 2
                    bb = bank[bk].ap().bitcast(BF16)
                    for k in range(n4):
                        e = half * 4 + k
                        if srcb is not None:
                            i_ap = srcb[:, e * 128:(e + 1) * 128]
                            idn, ridn = c_identb[:], R_identb
                        else:
                            i_ap = xa[sl][:, e * 128:(e + 1) * 128]
                            idn, ridn = c_ident[:], R_ident
                        O.op("tensor", lambda h, i_=i_ap,
                             o=(bb[:, k * 128:(k + 1) * 128] if srcb is not None
                                else bank[bk][:, k * 128:(k + 1) * 128]),
                             idn=idn: h.transpose(o, i_, idn),
                             reads=[rsrc, ridn], writes=[Rb[bk]])
                    dst = dstt[:, half * 4:half * 4 + n4,
                               tt * 128:(tt + 1) * 128]
                    if srcb is not None:
                        src_ap = bb[:, 0:n4 * 128]
                    else:
                        src_ap = bank[bk][:, 0:n4 * 128]
                    O.op("scalar", lambda h, o=dst, i_=src_ap
                         .rearrange("p (e m) -> p e m", m=128): h.copy(o, i_),
                         reads=[Rb[bk]], writes=[rdst])

        for s in range(B):
            es_at = contextlib.ExitStack()
            w1s = sb(es_at, f"s{s}w1s", [128, 4, 384])
            R_w1s = [Reg(f"w1s{i}") for i in range(4)]
            w1gh = es_at.enter_context(
                nc.sbuf_tensor(f"s{s}w1gh", [128, EC, 384], BF16))
            w1gl = es_at.enter_context(
                nc.sbuf_tensor(f"s{s}w1gl", [128, EC, 384], BF16))
            R_w1gh, R_w1gl = Reg("w1gh"), Reg("w1gl")
            sW1 = [O.new_sem(f"s{s}w1g{i}") for i in range(4)]
            qkv_tok = sb(es_at, f"s{s}qkv", [128, TT, 256])
            R_qkvt = [Reg(f"qkv{t}") for t in range(TT)]
            rope_tmp = sb(es_at, f"s{s}rtmp", [128, 128])
            R_rtmp = Reg("rtmp")
            sqscr = sb(es_at, f"s{s}sqs", [128, 64])
            R_sqscr = Reg("sqscr")
            vbuf = [sb(es_at, f"s{s}vb{i}", [128, TT, 130]) for i in range(2)]
            R_vbuf = [Reg("vbuf0"), Reg("vbuf1")]
            for i in range(2):
                O.op("gpsimd", lambda h, o=vbuf[i][:, :, 64:65]:
                     h.memset(o, 1.0), writes=[R_vbuf[i]])
                O.op("gpsimd", lambda h, o=vbuf[i][:, :, 129:130]:
                     h.memset(o, 1.0), writes=[R_vbuf[i]])
            qkh = {"q": es_at.enter_context(nc.sbuf_tensor(
                       f"s{s}qh", [128, T], BF16)),
                   "k": es_at.enter_context(nc.sbuf_tensor(
                       f"s{s}kh", [128, T], BF16))}
            qkl = {"q": es_at.enter_context(nc.sbuf_tensor(
                       f"s{s}ql", [128, T], BF16)),
                   "k": es_at.enter_context(nc.sbuf_tensor(
                       f"s{s}kl", [128, T], BF16))}
            R_qkh = {"q": [Reg("qh0"), Reg("qh1")],
                     "k": [Reg("kh0"), Reg("kh1")]}
            R_qkl = {"q": [Reg("ql0"), Reg("ql1")],
                     "k": [Reg("kl0"), Reg("kl1")]}
            ptmp = [sb(es_at, f"s{s}p{i}", [128, 512]) for i in range(3)]
            R_ptmp = [Reg(f"ptmp{i}") for i in range(3)]
            rc1 = sb(es_at, f"s{s}rc", [1, 2, 512])
            R_rc1 = [Reg("rc0"), Reg("rc1")]
            negcb = sb(es_at, f"s{s}ngc", [128, 2])
            R_negcb = [Reg("ngc0"), Reg("ngc1")]
            nsq = sb(es_at, f"s{s}nsq", [128, 4, TT])
            nmx = sb(es_at, f"s{s}nmx", [128, 4])
            stab = sb(es_at, f"s{s}stb", [4, 8])
            R_nsq, R_nmx, R_stab = Reg("nsq"), Reg("nmx"), Reg("stab")
            ostg = [sb(es_at, f"s{s}o{i}", [64, 512]) for i in range(2)]
            R_ostg = [Reg("ostg0"), Reg("ostg1")]
            sOS = [O.new_sem(f"s{s}ost0"), O.new_sem(f"s{s}ost1")]
            NQB = (T + 511) // 512

            if s == 0:
                for tt in range(TT):
                    emit_ln1_dve(0, tt)
                    if tt >= 1:
                        emit_ln1_pe(0, tt - 1)
                emit_ln1_pe(0, TT - 1)

            def emit_qkv_load_split(g):
                for e0 in range(EC):
                    ws = e0 % 4
                    src = w1t[e0 * 128:(e0 + 1) * 128,
                              g * 384:(g + 1) * 384].rearrange(
                        "(e p) m -> p e m", p=128)
                    O.op("sync", lambda h, o=w1s[:, ws:ws + 1, :], i_=src:
                         h.dma_start(out=o, in_=i_),
                         reads=[R_w1t], writes=[R_w1s[ws]], dma=sW1[ws])
                    O.op("vector", lambda h, o=w1gh[:, e0:e0 + 1, :],
                         a=w1s[:, ws:ws + 1, :]: h.tensor_copy(o, a),
                         reads=[R_w1s[ws]], writes=[R_w1gh])
                    O.op("vector", lambda h, o=w1gl[:, e0:e0 + 1, :],
                         a=w1s[:, ws:ws + 1, :], b=w1gh[:, e0:e0 + 1, :]:
                         h.tensor_tensor(o, a, b, ALU.subtract),
                         reads=[R_w1s[ws], R_w1gh], writes=[R_w1gl])

            def emit_qkv_tt(g, tt):
                bk = 6
                ops = []
                for e in range(EC):
                    ops.append((ln1h, R_ln1h, w1gh, R_w1gh, e))
                    ops.append((ln1h, R_ln1h, w1gl, R_w1gl, e))
                    ops.append((ln1l, R_ln1l, w1gh, R_w1gh, e))
                for i, (lt, rl, wt, rw, e) in enumerate(ops):
                    spv = (i == len(ops) - 1)
                    O.op("tensor", lambda h, o=bank[bk][:, 0:384],
                         l=lt[:, e, tt * 128:(tt + 1) * 128],
                         r=wt[:, e, :], st=(i == 0), sp=(i == len(ops) - 1):
                         h.matmul(o, l, r, start=st, stop=sp),
                         reads=[rl, rw], writes=[Rb[bk]], noinc=not spv)
                gv = g % 2
                O.op("scalar", lambda h, o=qkv_tok[:, tt, :],
                     i_=bank[bk][:, 0:256]: h.copy(o, i_),
                     reads=[Rb[bk]], writes=[R_qkvt[tt]])
                O.op("scalar", lambda h, o=vbuf[gv][:, tt, 0:64],
                     i_=bank[bk][:, 256:320]: h.copy(o, i_),
                     reads=[Rb[bk]], writes=[R_vbuf[gv]])
                O.op("scalar", lambda h, o=vbuf[gv][:, tt, 65:129],
                     i_=bank[bk][:, 320:384]: h.copy(o, i_),
                     reads=[Rb[bk]], writes=[R_vbuf[gv]])

            def emit_rope_tile(g, tt):
                for idx in range(4):
                    c0 = idx * 64
                    src = qkv_tok[:, tt, c0:c0 + 64]
                    t1 = rope_tmp[:, 0:64]
                    t2 = rope_tmp[:, 64:128]
                    O.op("vector", lambda h, o=t1, a=src,
                         b=c_cosT[:, tt, :]:
                         h.tensor_tensor(o, a, b, ALU.mult),
                         reads=[R_qkvt[tt], R_cosT], writes=[R_rtmp])
                    O.op("vector", lambda h, o=t2[:, 0:32],
                         a=qkv_tok[:, tt, c0 + 32:c0 + 64],
                         b=c_sinfT[:, tt, 0:32]:
                         h.tensor_tensor(o, a, b, ALU.mult),
                         reads=[R_qkvt[tt], R_sinfT], writes=[R_rtmp])
                    O.op("vector", lambda h, o=t2[:, 32:64],
                         a=qkv_tok[:, tt, c0:c0 + 32],
                         b=c_sinfT[:, tt, 32:64]:
                         h.tensor_tensor(o, a, b, ALU.mult),
                         reads=[R_qkvt[tt], R_sinfT], writes=[R_rtmp])
                    O.op("vector", lambda h, o=src, a=t1, b=t2:
                         h.tensor_tensor(o, a, b, ALU.add),
                         reads=[R_rtmp], writes=[R_qkvt[tt]])
            def emit_stab_sq(g, tt):
                for idx in range(4):
                    O.op("scalar", lambda h, o=sqscr[:],
                         a=qkv_tok[:, tt, idx * 64:(idx + 1) * 64],
                         m=nsq[:, idx, tt:tt + 1]:
                         h.activation(o, a, AF.Square, accum_out=m),
                         reads=[R_qkvt[tt]], writes=[R_sqscr, R_nsq])

            def emit_stab(g):
                gp = g % 2
                for idx in range(4):
                    O.op("vector", lambda h, o=nmx[:, idx:idx + 1],
                         a=nsq[:, idx, :]: h.tensor_reduce(o, a, AX.X, ALU.max),
                         reads=[R_nsq], writes=[R_nmx])
                O.op("tensor", lambda h, a=nmx[:, 0:4]: h.transpose(
                    bank[2][0:4, 0:128], a, c_ident[:]),
                    reads=[R_nmx, R_ident], writes=[Rb[2]])
                O.op("vector", lambda h, o=stab[0:4, 0:1]: h.tensor_reduce(
                    o, bank[2][0:4, 0:128], AX.X, ALU.max),
                    reads=[Rb[2]], writes=[R_stab])
                O.op("tensor", lambda h, a=stab[0:4, 0:1]: h.transpose(
                    bank[2][0:1, 0:4], a, c_ident[0:4, 0:4]),
                    reads=[R_stab, R_ident], writes=[Rb[2]])
                O.op("vector", lambda h, o=stab[0:1, 0:4]: h.tensor_copy(
                    o, bank[2][0:1, 0:4]),
                    reads=[Rb[2]], writes=[R_stab])
                O.op("vector", lambda h, o=stab[0:1, 4:5], a=stab[0:1, 0:1],
                     b=stab[0:1, 1:2]: h.tensor_tensor(o, a, b, ALU.mult),
                     reads=[R_stab], writes=[R_stab])
                O.op("vector", lambda h, o=stab[0:1, 5:6], a=stab[0:1, 2:3],
                     b=stab[0:1, 3:4]: h.tensor_tensor(o, a, b, ALU.mult),
                     reads=[R_stab], writes=[R_stab])
                O.op("vector", lambda h, o=stab[0:1, 4:5], a=stab[0:1, 4:5],
                     b=stab[0:1, 5:6]: h.tensor_tensor(o, a, b, ALU.max),
                     reads=[R_stab], writes=[R_stab])
                O.op("scalar", lambda h, o=stab[0:1, 4:5]: h.activation(
                    o, o, AF.Sqrt),
                    reads=[R_stab], writes=[R_stab])
                O.op("vector", lambda h, o=stab[0:1, 6:7], a=stab[0:1, 4:5]:
                     h.tensor_scalar(o, a, 0.1255, -40.0, ALU.mult, ALU.add),
                     reads=[R_stab], writes=[R_stab])
                O.op("scalar", lambda h, o=stab[0:1, 6:7]: h.activation(
                    o, o, AF.Relu),
                    reads=[R_stab], writes=[R_stab])
                O.op("vector", lambda h, o=stab[0:1, 6:7]: h.tensor_scalar(
                    o, o, -1.0, None, ALU.mult),
                    reads=[R_stab], writes=[R_stab])
                O.op("tensor", lambda h, r=stab[0:1, 6:7]: h.matmul(
                    bank[2][:, 0:1], c_ones[0:1, 0:128], r,
                    start=True, stop=True),
                    reads=[R_ones, R_stab], writes=[Rb[2]])
                O.op("scalar", lambda h, o=negcb[:, gp:gp + 1]:
                     h.copy(o, bank[2][:, 0:1]),
                     reads=[Rb[2]], writes=[R_negcb[gp]])

            def emit_qk_trans_group(g2, idx, t0):
                qk = "q" if idx % 2 == 0 else "k"
                half = idx // 2
                r0 = 64 * half
                hbuf, hreg = qkh[qk], R_qkh[qk][half]
                lbuf, lreg = qkl[qk], R_qkl[qk][half]
                c0 = idx * 64
                qsc = 0.125 if qk == "q" else None
                n4 = min(4, TT - t0)
                for k in range(n4):
                    tt = t0 + k
                    O.op("tensor", lambda h,
                         o=bank[6][0:64, k * 128:(k + 1) * 128],
                         i_=qkv_tok[:, tt, c0:c0 + 64]:
                         h.transpose(o, i_, c_ident[:]),
                         reads=[R_qkvt[tt], R_ident], writes=[Rb[6]])
                if qsc is None:
                    O.op("scalar", lambda h,
                         o=hbuf[r0:r0 + 64, t0 * 128:(t0 + n4) * 128],
                         i_=bank[6][0:64, 0:n4 * 128]: h.copy(o, i_),
                         reads=[Rb[6]], writes=[hreg])
                    O.op("vector", lambda h,
                         o=lbuf[r0:r0 + 64, t0 * 128:(t0 + n4) * 128],
                         a=bank[6][0:64, 0:n4 * 128],
                         b=hbuf[r0:r0 + 64, t0 * 128:(t0 + n4) * 128]:
                         h.tensor_tensor(o, a, b, ALU.subtract),
                         reads=[Rb[6], hreg], writes=[lreg])
                else:
                    O.op("scalar", lambda h,
                         o=hbuf[r0:r0 + 64, t0 * 128:(t0 + n4) * 128],
                         i_=bank[6][0:64, 0:n4 * 128], sc=qsc:
                         h.activation(o, i_, AF.Copy, bias=0.0,
                                      scale=sc),
                         reads=[Rb[6]], writes=[hreg])
                    O.op("vector", lambda h,
                         o=lbuf[r0:r0 + 64, t0 * 128:(t0 + n4) * 128],
                         a=bank[6][0:64, 0:n4 * 128],
                         b=hbuf[r0:r0 + 64, t0 * 128:(t0 + n4) * 128],
                         sc=qsc: h.scalar_tensor_tensor(
                             o, a, sc, b, ALU.mult, ALU.subtract),
                         reads=[Rb[6], hreg], writes=[lreg])

            att_chunks = []
            for kt in range(TT):
                for qb in range((kt * 128) // 512, NQB):
                    att_chunks.append((kt, qb))
            att_stops = {}
            for ci, (kt, qb) in enumerate(att_chunks):
                if kt == (min(T, qb * 512 + 512) - 1) // 128:
                    att_stops[ci] = qb

            def emit_attention_head(g, hh, fill, pend):
                h_abs = 2 * g + hh
                gp = g % 2
                nch = len(att_chunks)

                def s_group(ci):
                    kt, qb = att_chunks[ci]
                    q0 = qb * 512
                    q1 = min(T, q0 + 512)
                    qlo = max(q0, kt * 128)
                    w = q1 - qlo
                    bk = 4 + ci % 2
                    diag = (qlo == kt * 128)
                    r0 = 64 * hh
                    passes = ((qkh["k"], R_qkh["k"][hh],
                               qkh["q"], R_qkh["q"][hh]),
                              (qkh["k"], R_qkh["k"][hh],
                               qkl["q"], R_qkl["q"][hh]),
                              (qkl["k"], R_qkl["k"][hh],
                               qkh["q"], R_qkh["q"][hh]))
                    for i, (kb, rk, qv, rq) in enumerate(passes):
                        sp = (i == 2 and not diag)
                        O.op("tensor", lambda h, o=bank[bk][:, 0:w],
                             l=kb[r0:r0 + 64, kt * 128:(kt + 1) * 128],
                             r=qv[r0:r0 + 64, qlo:q1], st=(i == 0), sp=sp:
                             h.matmul(o, l, r, start=st, stop=sp),
                             reads=[rk, rq], writes=[Rb[bk]], noinc=not sp)
                    if diag:
                        O.op("tensor", lambda h, o=bank[bk][:, 0:128]:
                             h.matmul(o, c_identb[:], c_dmaskb[:],
                                      start=False, stop=True),
                             reads=[R_identb, R_dmaskb], writes=[Rb[bk]])
                    ps = ci % 3
                    O.op("scalar", lambda h, o=ptmp[ps][:, 0:w],
                         a=bank[bk][:, 0:w], ng=negcb[:, gp:gp + 1]:
                         h.activation(o, a, AF.Exp, bias=ng, scale=1.0),
                         reads=[Rb[bk], R_negcb[gp]],
                         writes=[R_ptmp[ps]])

                def pv(ci):
                    kt, qb = att_chunks[ci]
                    q0 = qb * 512
                    q1 = min(T, q0 + 512)
                    qlo = max(q0, kt * 128)
                    ps = ci % 3
                    sp = ci in att_stops
                    O.op("tensor", lambda h,
                         o=bank[qb][0:65, qlo - q0:q1 - q0],
                         l=vbuf[g % 2][:, kt, hh * 65:hh * 65 + 65],
                         r=ptmp[ps][:, 0:q1 - qlo],
                         st=(kt == 0), sp=sp:
                         h.matmul(o, l, r, start=st, stop=sp),
                         reads=[R_vbuf[g % 2], R_ptmp[ps]], writes=[Rb[qb]],
                         noinc=not sp)

                for ci in range(nch):
                    s_group(ci)
                    if ci >= 2:
                        pv(ci - 2)
                        if (ci - 2) in att_stops:
                            qb = att_stops[ci - 2]
                            ev_recip(qb)
                            pend.append([-1, qb, h_abs])
                    if ci % 2 == 1:
                        fill()
                    for it in pend:
                        if it[0] == 0:
                            ev_ocopy(it[1])
                        elif it[0] == 1:
                            ev_bcast(it[1])
                        elif it[0] == 2:
                            ev_out(it[1], it[2])
                        it[0] += 1
                    pend[:] = [it for it in pend if it[0] < 3]
                flush_pend(pend)
                for cj in (nch - 2, nch - 1):
                    if cj >= 0:
                        pv(cj)
                        if cj in att_stops:
                            ev_recip(att_stops[cj])
                            pend.append([-1, att_stops[cj], h_abs])

            def ev_recip(qb):
                w = min(T, qb * 512 + 512) - qb * 512
                sl = qb % 2
                O.op("vector", lambda h, o=rc1[0:1, sl, 0:w],
                     a=bank[qb][64:65, 0:w]: h.reciprocal(o, a),
                     reads=[Rb[qb]], writes=[R_rc1[sl]])

            def ev_bcast(qb):
                w = min(T, qb * 512 + 512) - qb * 512
                sl = qb % 2
                O.op("tensor", lambda h, o=bank[7][0:64, 0:w],
                     l=c_ones[0:1, 0:64], r=rc1[0:1, sl, 0:w]:
                     h.matmul(o, l, r, start=True, stop=True),
                     reads=[R_rc1[sl], R_ones], writes=[Rb[7]])

            def ev_ocopy(qb):
                q0 = qb * 512
                w = min(T, q0 + 512) - q0
                sl = qb % 2
                O.op("scalar", lambda h, o=ostg[sl][0:64, 0:w],
                     i_=bank[qb][0:64, 0:w]: h.copy(o, i_),
                     reads=[Rb[qb]], writes=[R_ostg[sl]])

            def ev_out(qb, h_abs):
                q0 = qb * 512
                w = min(T, q0 + 512) - q0
                sl = qb % 2
                O.op("vector", lambda h, o=ostg[sl][0:64, 0:w],
                     b=bank[7][0:64, 0:w]:
                     h.tensor_tensor(o, o, b, ALU.mult),
                     reads=[R_ostg[sl], Rb[7]], writes=[R_ostg[sl]])
                O.op("gpsimd", lambda h,
                     o=o_d[s, h_abs * 64:(h_abs + 1) * 64, q0:q0 + w],
                     i_=ostg[sl][0:64, 0:w]: h.dma_start(out=o, in_=i_),
                     reads=[R_ostg[sl]], writes=[R_od[s][h_abs]],
                     dma=sOS[sl])

            def flush_pend(pend):
                for it in pend:
                    st = it[0]
                    if st <= 0:
                        ev_ocopy(it[1])
                    if st <= 1:
                        ev_bcast(it[1])
                    ev_out(it[1], it[2])
                pend[:] = []

            def emit_attention_pair(g):
                fillB = [(g, idx, t0) for t0 in range(0, TT, 4)
                         for idx in (2, 3)]
                fillR = []
                if g + 1 < G:
                    fillR += [("qkv", tt) for tt in range(TT)]
                    fillR += [("ta", idx, t0) for t0 in range(0, TT, 4)
                              for idx in (0, 1)]

                def pop_job():
                    if fillB:
                        g2, idx, t0 = fillB.pop(0)
                        emit_qk_trans_group(g2, idx, t0)
                    elif fillR:
                        j = fillR.pop(0)
                        if j[0] == "qkv":
                            emit_qkv_tt(g + 1, j[1])
                            emit_stab_sq(g + 1, j[1])
                            emit_rope_tile(g + 1, j[1])
                        else:
                            emit_qk_trans_group(g + 1, j[1], j[2])

                pend = []
                emit_attention_head(g, 0, pop_job, pend)
                while fillB:
                    pop_job()
                emit_attention_head(g, 1, pop_job, pend)
                while fillR:
                    pop_job()
                flush_pend(pend)

            emit_qkv_load_split(0)
            for tt in range(TT):
                emit_qkv_tt(0, tt)
                emit_stab_sq(0, tt)
                emit_rope_tile(0, tt)
            for t0 in range(0, TT, 4):
                for idx in (0, 1):
                    emit_qk_trans_group(0, idx, t0)
            emit_stab(0)
            for g in range(G):
                if g + 1 < G:
                    emit_qkv_load_split(g + 1)
                emit_attention_pair(g)
                if g + 1 < G:
                    emit_stab(g + 1)
            O.barrier()
            es_at.close()

            es_w2 = contextlib.ExitStack()
            oblk = [sb(es_w2, f"s{s}ob{i}", [128, EC, 128]) for i in range(3)]
            R_oblk = [Reg(f"ob{i}") for i in range(3)]
            oblkh = [es_w2.enter_context(nc.sbuf_tensor(
                f"s{s}obh{i}", [128, EC, 128], BF16)) for i in range(2)]
            oblkl = [es_w2.enter_context(nc.sbuf_tensor(
                f"s{s}obl{i}", [128, EC, 128], BF16)) for i in range(2)]
            R_oblkh = [Reg("obh0"), Reg("obh1")]
            R_oblkl = [Reg("obl0"), Reg("obl1")]
            sOB = [O.new_sem(f"s{s}ob{i}") for i in range(3)]
            xw = [sb(es_w2, f"s{s}xw{i}", [128, E]) for i in range(2)]
            R_xw = [Reg("xw0"), Reg("xw1")]
            sXW = [O.new_sem(f"s{s}xw0"), O.new_sem(f"s{s}xw1")]
            x1t = [sb(es_w2, f"s{s}x1t{i}", [128, E]) for i in range(2)]
            R_x1t = [Reg("x1t0"), Reg("x1t1")]
            yt = [sb(es_w2, f"s{s}yt{i}", [128, E]) for i in range(2)]
            R_yt = [Reg("yt0"), Reg("yt1")]
            sX1 = [O.new_sem(f"s{s}x10"), O.new_sem(f"s{s}x11")]
            pdr = [sb(es_w2, f"s{s}pdr{i}", [128, max(E, I)])
                   for i in range(2)]
            pdrb = [es_w2.enter_context(
                nc.sbuf_tensor(f"s{s}pdrb{i}", [128, max(E, I)], BF16))
                for i in range(2)]
            pds = es_w2.enter_context(
                nc.sbuf_tensor(f"s{s}pds", [128, max(EC, IC) * 128], BF16))
            Rpdr = [Reg("pdr0"), Reg("pdr1")]
            Rpdrb = [Reg("pdrb0"), Reg("pdrb1")]
            Rpds = Reg("pds")

            npj = len(prep_jobs)
            jobs = prep_jobs[s * npj // B:(s + 1) * npj // B]
            njobs = len(jobs)
            for tt in range(TT):
                sl = tt % 2
                s4 = tt % 3
                gt = s * T + tt * 128
                src = o_d[s].rearrange("(e p) t -> p e t", p=128)[
                    :, :, tt * 128:(tt + 1) * 128]
                O.op("sync", lambda h, o=oblk[s4][:], i_=src:
                     h.dma_start(out=o, in_=i_),
                     reads=R_od[s], writes=[R_oblk[s4]], dma=sOB[s4])
                O.op("vector", lambda h, o=oblkh[sl][:], a=oblk[s4][:]:
                     h.tensor_copy(o, a),
                     reads=[R_oblk[s4]], writes=[R_oblkh[sl]])
                O.op("vector", lambda h, o=oblkl[sl][:], a=oblk[s4][:],
                     b=oblkh[sl][:]: h.tensor_tensor(o, a, b, ALU.subtract),
                     reads=[R_oblk[s4], R_oblkh[sl]], writes=[R_oblkl[sl]])
                O.op("sync", lambda h, o=xw[sl][:], i_=x_in[gt:gt + 128, :]:
                     h.dma_start(out=o, in_=i_),
                     reads=[R_x_in], writes=[R_xw[sl]], dma=sXW[sl])
                b0 = 2 * sl
                for half, (c0, cw) in enumerate(EBLK):
                    ops = []
                    for e in range(EC):
                        ops.append((oblkh[sl], R_oblkh[sl], w2sh, R_w2sh, e))
                        ops.append((oblkh[sl], R_oblkh[sl], w2sl, R_w2sl, e))
                        ops.append((oblkl[sl], R_oblkl[sl], w2sh, R_w2sh, e))
                    for i, (ob, ro, wt, rw, e) in enumerate(ops):
                        spv = (i == len(ops) - 1)
                        O.op("tensor", lambda h, o=bank[b0 + half][:, 0:cw],
                             l=ob[:, e, :],
                             r=wt[:, e, c0:c0 + cw],
                             st=(i == 0), sp=(i == len(ops) - 1):
                             h.matmul(o, l, r, start=st, stop=sp),
                             reads=[ro, rw], writes=[Rb[b0 + half]], noinc=not spv)
                    O.op("vector", lambda h,
                         o=x1t[sl][:, c0:c0 + cw],
                         a=bank[b0 + half][:, 0:cw],
                         b=xw[sl][:, c0:c0 + cw]:
                         h.tensor_tensor(o, a, b, ALU.add),
                         reads=[Rb[b0 + half], R_xw[sl]], writes=[R_x1t[sl]])
                ti = s * TT + tt
                O.op("gpsimd", lambda h, o=x1_d[gt:gt + 128, :], i_=x1t[sl][:]:
                     h.dma_start(out=o, in_=i_),
                     reads=[R_x1t[sl]], writes=[R_x1d[ti]], dma=sX1[sl])
                emit_layernorm(x1t[sl][:], R_x1t[sl], yt[sl][:], R_yt[sl])
                O.op("vector", lambda h, a=yt[sl][:]: h.tensor_reduce(
                    c_bn[:, 14:15], a, AX.X, ALU.max, apply_absolute_value=True),
                    reads=[R_yt[sl]], writes=[R_bn])
                O.op("vector", lambda h: h.tensor_tensor(
                    c_amax[:, 0:1], c_amax[:, 0:1], c_bn[:, 14:15], ALU.max),
                    reads=[R_bn, R_amax1], writes=[R_amax1])
                if s + 1 < B:
                    emit_ln1_dve(s + 1, tt)
                    if tt >= 1:
                        emit_ln1_pe(s + 1, tt - 1)
                    if tt == TT - 1:
                        emit_ln1_pe(s + 1, TT - 1)
                while jobs and (njobs - len(jobs)) < ((tt + 1) * njobs) // TT:
                    jj = (njobs - len(jobs)) % 2
                    (srcw, rsrc, wcols, ncc, dstd, rdst, c,
                     scd, rscd) = jobs.pop(0)
                    O.op("sync", lambda h, a=pdr[jj][:, 0:wcols], srcw=srcw:
                         h.dma_start(out=a, in_=srcw),
                         reads=[rsrc], writes=[Rpdr[jj]], dma=sPL[jj])
                    emit_wquant_units(pdr[jj][:, 0:wcols], Rpdr[jj])
                    emit_units_cast(pdr[jj][:, 0:wcols], Rpdr[jj],
                                    pdrb[jj][:, 0:wcols], Rpdrb[jj])
                    emit_trans_store(pdrb[jj], Rpdrb[jj], pds, Rpds, ncc,
                                     dstd, rdst, c, sPS[jj], scd, rscd)
            O.barrier()
            es_w2.close()
        es_ln1.close()
        es_w2keep.close()

        sCC = O.new_sem("cc")
        sCL = O.new_sem("ccl")
        sCS = O.new_sem("ccs")

        def emit_sync_scale(which, amax_col, R_amax_c, clip_col, sc_buf, R_sc):
            O.op("tensor", lambda h, o=bank[0][0:1, 0:128],
                 i_=c_amax[:, amax_col:amax_col + 1]:
                 h.transpose(o, i_, c_ident[:]),
                 reads=[R_amax_c, R_ident], writes=[Rb[0]])
            O.op("vector", lambda h: h.tensor_reduce(
                c_scal[0:1, 0:1], bank[0][0:1, 0:128], AX.X, ALU.max),
                reads=[Rb[0]], writes=[R_scal])
            O.op("vector", lambda h: h.tensor_copy(
                c_scal[0:1, 0:8], c_scal[0:1, 0:1].broadcast_to((1, 8))),
                reads=[R_scal], writes=[R_scal])
            O.op("gpsimd", lambda h, o=cc_in_d[which]: h.dma_start(
                out=o, in_=c_scal[0:1, 0:8]),
                reads=[R_scal], writes=[R_ccin[which]], dma=sCS)
            O.op("gpsimd", lambda h, i_=cc_in_d[which], o=cc_out_d[which]:
                 h.collective_compute(
                     "AllReduce", ALU.max,
                     replica_groups=[list(range(cfg.n_cores))],
                     ins=[i_], outs=[o]),
                 reads=[R_ccin[which]], writes=[R_ccout[which]], sem=(sCC, 1))
            O.op("sync", lambda h, i_=cc_out_d[which]: h.dma_start(
                out=c_scal[0:1, 2:4], in_=i_[0:1, 0:2]),
                reads=[R_ccout[which]], writes=[R_scal], dma=sCL)
            O.op("scalar", lambda h: h.activation(
                c_scal[0:1, 4:5], c_clips[0:1, clip_col:clip_col + 1], AF.Exp,
                bias=0.0, scale=-1.0),
                reads=[R_clips], writes=[R_scal])
            O.op("vector", lambda h: h.tensor_scalar(
                c_scal[0:1, 4:5], c_scal[0:1, 4:5], 1.0, None, ALU.add),
                reads=[R_scal], writes=[R_scal])
            O.op("vector", lambda h: h.reciprocal(
                c_scal[0:1, 4:5], c_scal[0:1, 4:5]),
                reads=[R_scal], writes=[R_scal])
            O.op("vector", lambda h: h.tensor_tensor(
                c_scal[0:1, 4:5], c_scal[0:1, 4:5], c_scal[0:1, 2:3], ALU.mult),
                reads=[R_scal], writes=[R_scal])
            O.op("vector", lambda h: h.tensor_scalar(
                c_scal[0:1, 5:6], c_scal[0:1, 4:5], C127, None, ALU.mult),
                reads=[R_scal], writes=[R_scal])
            O.op("vector", lambda h: h.reciprocal(
                c_scal[0:1, 6:7], c_scal[0:1, 4:5]),
                reads=[R_scal], writes=[R_scal])
            O.op("vector", lambda h: h.tensor_scalar(
                c_scal[0:1, 6:7], c_scal[0:1, 6:7], 127.0, None, ALU.mult),
                reads=[R_scal], writes=[R_scal])
            O.op("tensor", lambda h: h.matmul(
                bank[1][:, 0:2], c_ones[:], c_scal[0:1, 5:7],
                start=True, stop=True),
                reads=[R_ones, R_scal], writes=[Rb[1]])
            O.op("vector", lambda h, o=sc_buf[:, 0:2]: h.tensor_copy(
                o, bank[1][:, 0:2]),
                reads=[Rb[1]], writes=[R_sc])

        emit_sync_scale(0, 0, R_amax1, 0, c_sc1, R_sc1)

        es2 = contextlib.ExitStack()
        xqf = es2.enter_context(
            nc.sbuf_tensor("xqf", [128, EC, TTC * 128], BF16))
        R_xqf = Reg("xqf")
        yin = [sb(es2, f"yin{i}", [128, E]) for i in range(2)]
        R_yin = [Reg("yin0"), Reg("yin1")]
        sYI = [O.new_sem("yi0"), O.new_sem("yi1")]
        yinb = [es2.enter_context(
            nc.sbuf_tensor(f"yinb{i}", [128, E], BF16)) for i in range(2)]
        R_yinb = [Reg("yinb0"), Reg("yinb1")]
        wqg = [es2.enter_context(
            nc.sbuf_tensor(f"wqg{i}", [128, EC, 128], BF16)) for i in range(2)]
        wqu = [es2.enter_context(
            nc.sbuf_tensor(f"wqu{i}", [128, EC, 128], BF16)) for i in range(2)]
        R_wqg = [Reg("wqg0"), Reg("wqg1")]
        R_wqu = [Reg("wqu0"), Reg("wqu1")]
        sWG = [O.new_sem("wg0"), O.new_sem("wg1")]
        sWU = [O.new_sem("wu0"), O.new_sem("wu1")]
        sgl = [sb(es2, f"sgl{i}", [128, 512]) for i in range(2)]
        R_sgl = [Reg("sgl0"), Reg("sgl1")]
        hpc = sb(es2, "hpc", [128, TTC * 128])
        R_hpc = Reg("hpc")
        sHP = [O.new_sem(f"hpst{i}") for i in range(4)]
        wgssb = sb(es2, "wgssb", [128, IC])
        wussb = sb(es2, "wussb", [128, IC])
        R_wgss, R_wuss = Reg("wgss"), Reg("wuss")
        c_gu = sb(es2, "c_gu", [128, 4])
        R_gu = [Reg("gu0"), Reg("gu1")]
        sWS = O.new_sem("wssl")

        wsraw = sb(es2, "wsraw", [IC, 128])
        R_wsraw = Reg("wsraw")
        for (srcd, rsrcd, dst, rdst) in ((wgs_d, R_wgsd, wgssb, R_wgss),
                                         (wus_d, R_wusd, wussb, R_wuss)):
            O.op("sync", lambda h, o=wsraw[:], i_=srcd: h.dma_start(out=o, in_=i_),
                 reads=[rsrcd], writes=[R_wsraw], dma=sWS)
            O.op("tensor", lambda h, i_=wsraw[:]: h.transpose(
                bank[6][:, 0:IC], i_, c_ident[0:IC, 0:IC]),
                reads=[R_wsraw, R_ident], writes=[Rb[6]])
            O.op("vector", lambda h, o=dst[:]: h.tensor_copy(o, bank[6][:, 0:IC]),
                 reads=[Rb[6]], writes=[rdst])

        for ti in range(TTC):
            sl = ti % 2
            gt = ti * 128
            O.op("sync", lambda h, o=yin[sl][:], i_=x1_d[gt:gt + 128, :]:
                 h.dma_start(out=o, in_=i_),
                 reads=[R_x1d[ti]], writes=[R_yin[sl]], dma=sYI[sl])
            emit_layernorm(yin[sl][:], R_yin[sl], yin[sl][:], R_yin[sl])
            a = yin[sl][:]
            ab = yinb[sl][:]
            O.op("vector", lambda h, a=a: h.tensor_scalar(
                a.bitcast(I32), a, c_sc1[:, 1:2], None, ALU.mult),
                reads=[R_yin[sl], R_sc1], writes=[R_yin[sl]])
            emit_units_cast(a, R_yin[sl], ab, R_yinb[sl])
            for half in range((EC + 3) // 4):
                n4 = min(4, EC - half * 4)
                bk = 4 + half % 2
                bb = bank[bk].ap().bitcast(BF16)
                for k in range(n4):
                    e = half * 4 + k
                    O.op("tensor", lambda h, i_=yinb[sl][:, e * 128:(e + 1) * 128],
                         o=bb[:, k * 128:(k + 1) * 128]:
                         h.transpose(o, i_, c_identb[:]),
                         reads=[R_yinb[sl], R_identb], writes=[Rb[bk]])
                dst = xqf[:, half * 4:half * 4 + n4, gt:gt + 128]
                O.op("scalar", lambda h, o=dst, i_=bb[:, 0:n4 * 128]
                     .rearrange("p (e m) -> p e m", m=128): h.copy(o, i_),
                     reads=[Rb[bk]], writes=[R_xqf])

        NBLK = TTC * 128 // 512
        for c in range(IC):
            sl = c % 2
            O.op("sync", lambda h, o=wqg[sl][:],
                 i_=wgq_d[:, c * 128:(c + 1) * 128]
                 .rearrange("(e p) m -> p e m", p=128):
                 h.dma_start(out=o, in_=i_),
                 reads=[R_wgqd[c]], writes=[R_wqg[sl]], dma=sWG[sl])
            O.op("sync", lambda h, o=wqu[sl][:],
                 i_=wuq_d[:, c * 128:(c + 1) * 128]
                 .rearrange("(e p) m -> p e m", p=128):
                 h.dma_start(out=o, in_=i_),
                 reads=[R_wuqd[c]], writes=[R_wqu[sl]], dma=sWU[sl])
            O.op("vector", lambda h, o=c_gu[:, 2 * sl:2 * sl + 1],
                 a=wgssb[:, c:c + 1]: h.tensor_tensor(
                     o, a, c_sc1[:, 0:1], ALU.mult),
                 reads=[R_wgss, R_sc1], writes=[R_gu[sl]])
            O.op("vector", lambda h, o=c_gu[:, 2 * sl + 1:2 * sl + 2],
                 a=wussb[:, c:c + 1]: h.tensor_tensor(
                     o, a, c_sc1[:, 0:1], ALU.mult),
                 reads=[R_wuss, R_sc1], writes=[R_gu[sl]])
            for b in range(NBLK):
                bg = b % 2
                bu = 2 + b % 2
                for e in range(EC):
                    spv = (e == EC - 1)
                    O.op("tensor", lambda h, o=bank[bg][:, 0:512],
                         l=wqg[sl][:, e, :], r=xqf[:, e, b * 512:(b + 1) * 512],
                         st=(e == 0), sp=(e == EC - 1):
                         h.matmul(o, l, r, start=st, stop=sp),
                         reads=[R_wqg[sl], R_xqf], writes=[Rb[bg]], noinc=not spv)
                for e in range(EC):
                    spv = (e == EC - 1)
                    O.op("tensor", lambda h, o=bank[bu][:, 0:512],
                         l=wqu[sl][:, e, :], r=xqf[:, e, b * 512:(b + 1) * 512],
                         st=(e == 0), sp=(e == EC - 1):
                         h.matmul(o, l, r, start=st, stop=sp),
                         reads=[R_wqu[sl], R_xqf], writes=[Rb[bu]], noinc=not spv)
                O.op("scalar", lambda h, o=sgl[b % 2][:], a=bank[bg][:, 0:512],
                     sc=c_gu[:, 2 * sl:2 * sl + 1]:
                     h.activation(o, a, AF.Silu, bias=0.0, scale=sc),
                     reads=[Rb[bg], R_gu[sl]], writes=[R_sgl[b % 2]])
                O.op("vector", lambda h, o=hpc[:, b * 512:(b + 1) * 512],
                     a=bank[bu][:, 0:512], sg=sgl[b % 2][:],
                     sc=c_gu[:, 2 * sl + 1:2 * sl + 2]:
                     h.scalar_tensor_tensor(o, a, sc, sg, ALU.mult, ALU.mult),
                     reads=[Rb[bu], R_gu[sl], R_sgl[b % 2]], writes=[R_hpc])
                O.op("vector", lambda h, a=hpc[:, b * 512:(b + 1) * 512]:
                     h.tensor_reduce(c_bn[:, 14:15], a, AX.X, ALU.max,
                                     apply_absolute_value=True),
                     reads=[R_hpc], writes=[R_bn])
                O.op("vector", lambda h: h.tensor_tensor(
                    c_amax[:, 1:2], c_amax[:, 1:2], c_bn[:, 14:15], ALU.max),
                    reads=[R_bn, R_amax2], writes=[R_amax2])
            O.op("gpsimd", lambda h, o=hp_d[c], i_=hpc[:]:
                 h.dma_start(out=o, in_=i_),
                 reads=[R_hpc], writes=[R_hpd[c]], dma=sHP[c % 4])

        O.barrier()
        es2.close()

        es3 = contextlib.ExitStack()
        wqd = es3.enter_context(
            nc.sbuf_tensor("wqd", [128, IC, E], BF16))
        R_wqd = Reg("wqd")
        sWD = O.new_sem("wqd")
        HW3 = 512
        hps = [sb(es3, f"hps{i}", [128, HW3]) for i in range(4)]
        R_hps = [Reg(f"hps{i}") for i in range(4)]
        sHT = [O.new_sem(f"ht{i}") for i in range(8)]
        hptb = [es3.enter_context(
            nc.sbuf_tensor(f"hptb{i}", [128, IC, HW3], BF16)) for i in range(2)]
        R_hptb = [Reg("hptb0"), Reg("hptb1")]
        dwn = [sb(es3, f"dwn{i}", [128, EC, HW3]) for i in range(2)]
        R_dwn = [Reg("dwn0"), Reg("dwn1")]
        x1in = [sb(es3, f"x1in{i}", [128, E]) for i in range(4)]
        R_x1in = [Reg(f"x1i{i}") for i in range(4)]
        sX1I = [O.new_sem(f"x1i{i}") for i in range(4)]
        outt = [sb(es3, f"outt{i}", [128, E]) for i in range(4)]
        R_outt = [Reg(f"otr{i}") for i in range(4)]
        sOT = [O.new_sem(f"ot{i}") for i in range(4)]
        wdssb = sb(es3, "wdssb", [128, EC])
        R_wdss = Reg("wdss")
        c_ds = sb(es3, "c_ds", [128, EC])
        R_ds = Reg("cds")
        wsraw3 = sb(es3, "wsraw3", [EC, 128])
        R_wsraw3 = Reg("wsraw3")

        O.op("sync", lambda h, o=wqd[:],
             i_=wdq_d.rearrange("(c p) e -> p c e", p=128):
             h.dma_start(out=o, in_=i_),
             reads=R_wdqd, writes=[R_wqd], dma=sWD)
        O.op("sync", lambda h, o=wsraw3[:], i_=wds_d: h.dma_start(out=o, in_=i_),
             reads=[R_wdsd], writes=[R_wsraw3], dma=sWD)
        emit_sync_scale(1, 1, R_amax2, 1, c_sc2, R_sc2)
        O.op("tensor", lambda h: h.transpose(
            bank[6][:, 0:EC], wsraw3[:], c_ident[0:EC, 0:EC]),
            reads=[R_wsraw3, R_ident], writes=[Rb[6]])
        O.op("vector", lambda h: h.tensor_copy(wdssb[:], bank[6][:, 0:EC]),
             reads=[Rb[6]], writes=[R_wdss])
        O.op("vector", lambda h: h.tensor_scalar(
            c_ds[:], wdssb[:], c_sc2[:, 0:1], None, ALU.mult),
            reads=[R_wdss, R_sc2], writes=[R_ds])

        NHG = TTC * 128 // HW3
        for hg in range(NHG):
            sl = hg % 2
            for c in range(IC):
                cs = c % 4
                O.op("sync", lambda h, o=hps[cs][:],
                     i_=hp_d[c][:, hg * HW3:(hg + 1) * HW3]:
                     h.dma_start(out=o, in_=i_),
                     reads=[R_hpd[c]], writes=[R_hps[cs]], dma=sHT[c % 8])
                O.op("vector", lambda h, a=hps[cs][:]: h.tensor_scalar(
                    a.bitcast(I32), a, c_sc2[:, 1:2], None, ALU.mult),
                    reads=[R_hps[cs], R_sc2], writes=[R_hps[cs]])
                emit_units_cast(hps[cs][:], R_hps[cs],
                                hptb[sl][:, c, :], R_hptb[sl])
            for oc in range(EC):
                bk = oc % 4
                for c in range(IC):
                    spv = (c == IC - 1)
                    O.op("tensor", lambda h, o=bank[bk][:, 0:HW3],
                         l=wqd[:, c, oc * 128:(oc + 1) * 128],
                         r=hptb[sl][:, c, :],
                         st=(c == 0), sp=(c == IC - 1):
                         h.matmul(o, l, r, start=st, stop=sp),
                         reads=[R_hptb[sl], R_wqd], writes=[Rb[bk]], noinc=not spv)
                O.op("vector", lambda h, o=dwn[sl][:, oc, :], a=bank[bk][:, 0:HW3],
                     sc=c_ds[:, oc:oc + 1]: h.tensor_scalar(
                         o, a, sc, None, ALU.mult),
                     reads=[Rb[bk], R_ds], writes=[R_dwn[sl]])
            for t4 in range(HW3 // 128):
                ti = hg * (HW3 // 128) + t4
                gt = ti * 128
                xsl = ti % 4
                O.op("sync", lambda h, o=x1in[xsl][:], i_=x1_d[gt:gt + 128, :]:
                     h.dma_start(out=o, in_=i_),
                     reads=[R_x1d[ti]], writes=[R_x1in[xsl]], dma=sX1I[xsl])
                for half in range((EC + 3) // 4):
                    n4 = min(4, EC - half * 4)
                    bk = 4 + half % 2
                    for k in range(n4):
                        oc = half * 4 + k
                        O.op("tensor", lambda h,
                             i_=dwn[sl][:, oc, t4 * 128:(t4 + 1) * 128],
                             o=bank[bk][:, k * 128:(k + 1) * 128]:
                             h.transpose(o, i_, c_ident[:]),
                             reads=[R_dwn[sl], R_ident], writes=[Rb[bk]])
                    O.op("vector", lambda h,
                         o=outt[xsl][:, half * 512:half * 512 + n4 * 128],
                         a=bank[bk][:, 0:n4 * 128],
                         b=x1in[xsl][:, half * 512:half * 512 + n4 * 128]:
                         h.tensor_tensor(o, a, b, ALU.add),
                         reads=[Rb[bk], R_x1in[xsl]], writes=[R_outt[xsl]])
                O.op("gpsimd", lambda h, o=out_d[gt:gt + 128, :],
                     i_=outt[xsl][:]: h.dma_start(out=o, in_=i_),
                     reads=[R_outt[xsl]], writes=[R_out], dma=sOT[xsl])
        es3.close()

        O.final_wait("gpsimd", sOT)

        with nc.Block() as block:
            O.replay(block)
    return nc


def _host_constants(cfg):
    T, D = cfg.T, cfg.D
    inv = 1.0 / (10000.0 ** (np.arange(0, D, 2, dtype=np.float32) / D))
    ang = np.arange(T, dtype=np.float32)[:, None] * inv[None, :]
    cos = np.concatenate([np.cos(ang), np.cos(ang)], axis=1).astype(np.float32)
    sin = np.concatenate([np.sin(ang), np.sin(ang)], axis=1).astype(np.float32)
    sinf = sin.copy()
    sinf[:, : D // 2] *= -1.0
    ident = np.eye(128, dtype=np.float32)
    dmask = np.tril(np.full((128, 128), MASKVAL, dtype=np.float32), k=-1)
    return cos, sinf, ident, dmask


def _host_w1_reorder(W1, cfg):
    E = cfg.E
    perm = []
    for g in range(cfg.G):
        for hh in (2 * g, 2 * g + 1):
            perm.extend(range(hh * 64, hh * 64 + 64))
            perm.extend(range(E + hh * 64, E + hh * 64 + 64))
        for hh in (2 * g, 2 * g + 1):
            perm.extend(range(2 * E + hh * 64, 2 * E + hh * 64 + 64))
    return np.ascontiguousarray(W1[np.array(perm), :].T)


def make_in_maps(cfg, X, W1, W2, wg, wu, wd, clip_x, clip_down):
    cos, sinf, ident, dmask = _host_constants(cfg)
    w1t = _host_w1_reorder(np.asarray(W1, np.float32), cfg)
    w2t = np.ascontiguousarray(np.asarray(W2, np.float32).T)
    clips = np.array([[float(clip_x), float(clip_down)]], np.float32)
    in_maps = []
    for c in range(cfg.n_cores):
        xs = np.ascontiguousarray(
            X[c * cfg.B_loc:(c + 1) * cfg.B_loc].reshape(
                cfg.B_loc * cfg.T, cfg.E))
        in_maps.append({
            "x": xs,
            "w1t": w1t, "w2t": w2t,
            "wg": np.ascontiguousarray(np.asarray(wg, np.float32)),
            "wu": np.ascontiguousarray(np.asarray(wu, np.float32)),
            "wd": np.ascontiguousarray(np.asarray(wd, np.float32)),
            "clips": clips, "cost": cos, "sinft": sinf,
            "ident": ident, "dmask": dmask,
        })
    return in_maps


def kernel(L0_attn_x, L0_attn_W1, L0_attn_W2, L0_ffn_clip_x, L0_ffn_clip_down,
           L0_ffn_w_gate, L0_ffn_w_up, L0_ffn_w_down):
    cfg = Cfg()
    X = np.ascontiguousarray(np.asarray(L0_attn_x, dtype=np.float32))
    nc = build(cfg)
    in_maps = make_in_maps(cfg, X, L0_attn_W1, L0_attn_W2,
                           L0_ffn_w_gate, L0_ffn_w_up, L0_ffn_w_down,
                           np.asarray(L0_ffn_clip_x)[0],
                           np.asarray(L0_ffn_clip_down)[0])
    res = run_bass_kernel_spmd(nc, in_maps, core_ids=list(range(cfg.n_cores)))
    outs = [r["out"].reshape(cfg.B_loc, cfg.T, cfg.E) for r in res.results]
    return np.concatenate(outs, axis=0)
